# version 07
# baseline (speedup 1.0000x reference)
import sys

sys.path.insert(0, "/opt/trn_rl_repo")
from contextlib import ExitStack

import numpy as np

import concourse.bass as bass
import concourse.mybir as mybir
from concourse.bass_utils import run_bass_kernel_spmd

F32 = mybir.dt.float32
BF16 = mybir.dt.bfloat16
U8 = mybir.dt.uint8
AF = mybir.ActivationFunctionType
ALU = mybir.AluOpType
AX = mybir.AxisListType

S = 4096
D = 2048
I = 8192
NG = 8
GT = 512
TPG = 4
NT = NG * TPG
DC = D // 128
EPS = 1e-5
INV_D = 1.0 / D
DEBUG_DUMP = False


class Sems:
    def __init__(self, nc, stack):
        self.nc, self.stack = nc, stack
        self.h, self.total, self.maxwait = {}, {}, {}

    def new(self, name):
        self.h[name] = self.stack.enter_context(self.nc.semaphore(name))
        self.total[name] = 0
        self.maxwait[name] = 0

    def inc(self, ins, name, n):
        self.total[name] += n
        return ins.then_inc(self.h[name], n)

    def wait(self, eng, name, val):
        if val <= 0:
            return
        self.maxwait[name] = max(self.maxwait[name], val)
        eng.wait_ge(self.h[name], val)

    def check(self):
        bad = {k: (self.maxwait[k], self.total[k])
               for k in self.total if self.maxwait[k] > self.total[k]}
        assert not bad, f"waits exceed emitted totals: {bad}"


def build_program():
    nc = bass.Bass("TRN2", target_bir_lowering=False, debug=False,
                   detect_race_conditions=False)

    X_d = nc.dram_tensor("X", [S, D], F32, kind="ExternalInput").ap()
    Wq_d = nc.dram_tensor("Wq", [D, D], F32, kind="ExternalInput").ap()
    Wkv_d = nc.dram_tensor("Wkv", [D, 2 * D], F32, kind="ExternalInput").ap()
    Wo_d = nc.dram_tensor("Wo", [D, D], F32, kind="ExternalInput").ap()
    Wg_d = nc.dram_tensor("Wg", [D, I], F32, kind="ExternalInput").ap()
    Wu_d = nc.dram_tensor("Wu", [D, I], F32, kind="ExternalInput").ap()
    Wd_d = nc.dram_tensor("Wd", [I, D], F32, kind="ExternalInput").ap()
    bqfm_d = nc.dram_tensor("bq_fm", [128, DC], F32, kind="ExternalInput").ap()
    bkvfm_d = nc.dram_tensor("bkv_fm", [128, 2 * DC], F32, kind="ExternalInput").ap()
    borep_d = nc.dram_tensor("bo_rep", [128, D], F32, kind="ExternalInput").ap()
    garep_d = nc.dram_tensor("ga_rep", [128, D], F32, kind="ExternalInput").ap()
    berep_d = nc.dram_tensor("be_rep", [128, D], F32, kind="ExternalInput").ap()
    cos_d = nc.dram_tensor("cos_fm", [512, S], F32, kind="ExternalInput").ap()
    sin_d = nc.dram_tensor("sin_fm", [512, S], F32, kind="ExternalInput").ap()
    OUT_d = nc.dram_tensor("OUT", [S, D], F32, kind="ExternalOutput").ap()

    wq_bf = nc.dram_tensor("wq_bf", [D, D], BF16, kind="Internal").ap()
    wkv_bf = nc.dram_tensor("wkv_bf", [D, 2 * D], BF16, kind="Internal").ap()
    wo_bf = nc.dram_tensor("wo_bf", [D, D], BF16, kind="Internal").ap()
    wg_bf = nc.dram_tensor("wg_bf", [D, I], BF16, kind="Internal").ap()
    wu_bf = nc.dram_tensor("wu_bf", [D, I], BF16, kind="Internal").ap()
    wd_bf = nc.dram_tensor("wd_bf", [I, D], BF16, kind="Internal").ap()
    xn_sp = nc.dram_tensor("xn_sp", [S, D], BF16, kind="Internal").ap()
    x2n_sp = nc.dram_tensor("x2n_sp", [S, D], BF16, kind="Internal").ap()
    x2_sp = nc.dram_tensor("x2_sp", [S, D], F32, kind="Internal").ap()
    o3_sp = nc.dram_tensor("o3_sp", [D, GT], BF16, kind="Internal").ap()
    if DEBUG_DUMP:
        dbg_xn = nc.dram_tensor("dbg_xn", [S, D], BF16, kind="ExternalOutput").ap()
        dbg_x2 = nc.dram_tensor("dbg_x2", [S, D], F32, kind="ExternalOutput").ap()
        dbg_x2n = nc.dram_tensor("dbg_x2n", [S, D], BF16, kind="ExternalOutput").ap()
        dbg_o3 = nc.dram_tensor("dbg_o3", [D, GT], BF16, kind="ExternalOutput").ap()
        dbg_ar = nc.dram_tensor("dbg_ar", [128, 212736], U8, kind="ExternalOutput").ap()

    with ExitStack() as stack:
        e = stack.enter_context
        sem = Sems(nc, stack)

        ARENA = 212736
        ar8 = e(nc.sbuf_tensor("arena", [128, ARENA], U8))
        arf = ar8.bitcast(F32)
        arb = ar8.bitcast(BF16)

        def vf(off, cols, parts=128):
            assert off % 4 == 0
            return arf[0:parts, off // 4: off // 4 + cols]

        def vb(off, cols, parts=128):
            assert off % 2 == 0
            return arb[0:parts, off // 2: off // 2 + cols]

        cstv = vf(0, 8)
        CONSTS = [0.0, 1.0, EPS, INV_D, -0.5]
        for ci, cvv in enumerate(CONSTS):
            nc.const_aps.register(F32, cvv, cstv[:, ci:ci + 1])
        A_fm = vf(32, DC)
        Abq = vf(96, DC)
        bq_fm = vf(160, DC)
        bkv_fm = vf(224, 2 * DC)
        stats = vf(352, 2 * NT)
        onesc = vf(608, 1)
        onesb = vb(24, 1)
        ones1 = vf(640, 128, parts=1)
        scrS = vf(1152, 1)
        mu2s = vf(1184, 1)
        r2s = vf(1216, 1)
        tl_s = vf(1248, 1)
        Agrp = vf(1280, 16 * NG)
        nk_sb = vf(1792, GT, parts=1)
        rn_sb = vf(3840, GT, parts=1)
        rnbc = vf(5888, GT)
        ga_rep = vb(7936, D)
        be_rep = vb(12032, D)
        sg_sb = vf(7936, 4 * GT)
        ah = sg_sb
        Xb = [vf(16128, D), vf(24320, D)]
        x2r = [vf(16128, D), vf(24320, D)]
        xc = vf(32512, D)
        o3tm = [vb(32512, D), vb(36608, D)]
        sqb = vf(40704, D)
        dcmp = [vb(40704 + 1024 * i, GT) for i in range(6)]
        srank = [vb(46848, GT), vb(47872, GT), vb(48896, GT), vb(49920, GT)]
        xn_tm = [vb(48896, D), vb(52992, D)]
        x2n_tm = xn_tm
        o3bf = vb(50944, 2048)
        Xn_fm = [vb(57088, DC * GT), vb(73472, DC * GT)]
        x2n_fm = vb(57088, DC * GT)
        wd_rg = [vb(73472, 4 * 1024), vb(81664, 4 * 1024)]
        wkv_rg = [vb(89856, DC * 256), vb(204544, DC * 256)]
        wq_rg = wkv_rg
        h_f = vf(89856, 4 * GT)
        cos_sb = vf(98048, 4 * GT)
        sin_sb = vf(106240, 4 * GT)
        x2b = [vf(98048, D), vf(106240, D)]
        wgu0 = vb(98048, DC * GT)
        t1 = vf(114432, 4 * GT)
        t2 = vf(122624, 4 * GT)
        sqn = [t1, t2]
        sqnb = [vb(114432, 4 * GT), vb(122624, 4 * GT)]
        Y_fm = vb(114432, DC * GT)
        wgu1 = vb(114432, DC * GT)
        K_fm = vf(130816, DC * GT)
        V_fm = vf(163584, DC * GT)
        wo_rg = [vb(130816, 4 * D), vb(147200, 4 * D)]
        bo_rep = vb(163584, D)
        h_s = vb(130816, (I // 128) * GT)
        wgu2 = vb(196352, DC * GT)
        pcst = vb(196352, 4096)
        wgu = [wgu0, wgu1, wgu2]

        for n in ["c_kv", "c_q", "c_o", "c_g", "c_u", "c_d",
                  "x1a", "x1b", "xf1", "ln", "sp1a", "sp1b", "fm1a", "fm1b",
                  "wkva", "wkvb", "kvmm", "kvev",
                  "csld", "cgb", "cssem", "sqv", "nmm", "rnv", "rnbc", "pa", "afin",
                  "vstat", "sstd", "vself",
                  "x2la", "x2lb", "wqa", "wqb", "woa", "wob2", "wob", "qmm", "yev",
                  "omm", "ev2", "v3", "sc3", "sp2a", "sp2b",
                  "fm3", "wgu0", "wgu1", "wgu2", "gmm", "umm", "silu", "hdn", "spar",
                  "dmm", "dmk", "o3ev", "o3sp", "o3t00", "o3t01", "o3t10", "o3t11",
                  "x2ra", "x2rb", "fin", "outwa", "outwb"]:
            sem.new(n)

        ps_t = e(nc.psum_tensor("ps", [128, 4096], F32))
        ps = ps_t.ap()

        def bank(b, w=1):
            return ps[:, b * 512:(b + w) * 512]

        block = e(nc.Block())

        def chunks_of(rows, cols):
            cb = min(cols, 4096)
            rb = max(128, (4096 // cb) * 128)
            return [(r, min(rb, rows - r), c, cb)
                    for r in range(0, rows, rb) for c in range(0, cols, cb)]

        kv_ch = chunks_of(D, 2 * D)
        q_ch = chunks_of(D, D)
        o_ch = chunks_of(D, D)
        g_ch = chunks_of(D, I)
        u_ch = chunks_of(D, I)
        d_ch = chunks_of(I, D)
        CKV = 32 * len(kv_ch)
        CQ = 32 * len(q_ch)
        CO = 32 * len(o_ch)
        CG = 32 * len(g_ch)
        CU = 32 * len(u_ch)
        CD = 32 * len(d_ch)

        @block.gpsimd
        def _(gp):
            ins = gp.dma_start(out=ga_rep, in_=garep_d)
            sem.inc(ins, "cgb", 16)
            sem.wait(gp, "cgb", sem.total["cgb"])
            ins = gp.dma_start(out=be_rep, in_=berep_d)
            sem.inc(ins, "cgb", 16)
            sem.wait(gp, "cgb", sem.total["cgb"])

            def do_cast(w32, wbf, chunks, sname):
                for (r, rp, c, cp) in chunks:
                    nb = rp // 128
                    src = w32[r:r + rp, c:c + cp].rearrange("(b p) c2 -> p b c2", p=128)
                    dst = pcst[:, 0:nb * cp].rearrange("p (b c2) -> p b c2", b=nb)
                    ins = gp.dma_start(out=dst, in_=src)
                    sem.inc(ins, sname, 16)
                    sem.wait(gp, sname, sem.total[sname])
                    dd = wbf[r:r + rp, c:c + cp].rearrange("(b p) c2 -> p b c2", p=128)
                    ins = gp.dma_start(out=dd, in_=dst)
                    sem.inc(ins, sname, 16)
                    sem.wait(gp, sname, sem.total[sname])

            do_cast(Wkv_d, wkv_bf, kv_ch, "c_kv")
            do_cast(Wq_d, wq_bf, q_ch, "c_q")
            do_cast(Wo_d, wo_bf, o_ch, "c_o")
            do_cast(Wg_d, wg_bf, g_ch, "c_g")
            do_cast(Wu_d, wu_bf, u_ch, "c_u")
            do_cast(Wd_d, wd_bf, d_ch, "c_d")
            sem.wait(gp, "pa", NG)
            ins = gp.dma_start(out=bo_rep, in_=borep_d)
            sem.inc(ins, "cgb", 16)
            sem.wait(gp, "cgb", sem.total["cgb"])

        @block.sync
        def _(sy):
            ins = sy.dma_start(out=bq_fm, in_=bqfm_d)
            sem.inc(ins, "csld", 16)
            ins = sy.dma_start(out=bkv_fm, in_=bkvfm_d)
            sem.inc(ins, "csld", 16)

            def xload1(t):
                sem.wait(sy, "xf1", t - 1)
                ins = sy.dma_start(out=Xb[t % 2], in_=X_d[t * 128:(t + 1) * 128, :])
                sem.inc(ins, "x1a" if t % 2 == 0 else "x1b", 16)

            def wkvload(g, j):
                J = 16 * g + j
                if J >= 2:
                    sem.wait(sy, "kvmm", 2 * J - 2)
                src = wkv_bf[:, j * 256:(j + 1) * 256].rearrange("(k p) c -> p k c", p=128)
                ins = sy.dma_start(out=wkv_rg[J % 2].rearrange("p (k c) -> p k c", k=DC), in_=src)
                sem.inc(ins, "wkva" if J % 2 == 0 else "wkvb", 16)

            def spill1(t):
                sem.wait(sy, "ln", t + 1)
                ins = sy.dma_start(out=xn_sp[t * 128:(t + 1) * 128, :], in_=xn_tm[t % 2])
                sem.inc(ins, "sp1a" if t % 2 == 0 else "sp1b", 16)

            xload1(0)
            xload1(1)
            for ti in range(TPG):
                if ti + 2 < TPG:
                    xload1(ti + 2)
                spill1(ti)
            sem.wait(sy, "c_kv", CKV)
            for g in range(NG):
                for ti in range(TPG):
                    if g + 1 < NG:
                        xload1(4 * (g + 1) + ti)
                        if ti >= 1:
                            spill1(4 * (g + 1) + ti - 1)
                        if ti == 3:
                            spill1(4 * (g + 1) + 3)
                    if ti == 0:
                        sem.wait(sy, "sp1a", 16 * (2 * g + 2))
                        sem.wait(sy, "sp1b", 16 * (2 * g + 2))
                        sem.wait(sy, "kvmm", 32 * (g - 1))
                        for c in range(DC // 2, DC):
                            ins = sy.dma_start_transpose(
                                Xn_fm[g % 2][:, c * GT:(c + 1) * GT],
                                xn_sp[g * GT:(g + 1) * GT, c * 128:(c + 1) * 128])
                            sem.inc(ins, "fm1a" if g % 2 == 0 else "fm1b", 16)
                    for j in range(4 * ti, 4 * ti + 4):
                        wkvload(g, j)

            for g in range(NG):
                if g == 0:
                    sem.wait(sy, "c_q", CQ)
                    sem.wait(sy, "c_o", CO)
                if g < 2:
                    sem.wait(sy, "kvmm", 32 * (NG - 1 + g))
                else:
                    sem.wait(sy, "qmm", 16 * (g - 1))
                sem.wait(sy, "sp1a", 16 * (NT // 2))
                sem.wait(sy, "sp1b", 16 * (NT // 2))
                for c in range(DC // 2, DC):
                    ins = sy.dma_start_transpose(
                        Xn_fm[g % 2][:, c * GT:(c + 1) * GT],
                        xn_sp[g * GT:(g + 1) * GT, c * 128:(c + 1) * 128])
                    sem.inc(ins, "fm1a" if g % 2 == 0 else "fm1b", 16)
                for j in range(8):
                    J = 8 * g + j
                    if J < 2:
                        sem.wait(sy, "kvmm", 32 * NG)
                    else:
                        sem.wait(sy, "qmm", 2 * J - 2)
                    src = wq_bf[:, j * 256:(j + 1) * 256].rearrange("(k p) c -> p k c", p=128)
                    ins = sy.dma_start(out=wq_rg[J % 2].rearrange("p (k c) -> p k c", k=DC), in_=src)
                    sem.inc(ins, "wqa" if J % 2 == 0 else "wqb", 16)
                for ti in range(2):
                    t = 4 * g + ti
                    if t >= 2:
                        sem.wait(sy, "ev2", 8 * ((t - 2) // 4) + [1, 2, 4, 6][(t - 2) % 4])
                    ins = sy.dma_start(out=Xb[t % 2], in_=X_d[t * 128:(t + 1) * 128, :])
                    sem.inc(ins, "x2la" if t % 2 == 0 else "x2lb", 16)
                for j in [0]:
                    J = 8 * g + j
                    if J < 2:
                        sem.wait(sy, "pa", NG)
                    else:
                        ref = J - 2
                        if ref % 4 == 3:
                            sem.wait(sy, "omm", 2 * (ref // 4) + 2)
                        else:
                            sem.wait(sy, "wob", 3 * (ref // 4) + ref % 4 + 1)
                    jj = j % 4
                    src = wo_bf[jj * 512:(jj + 1) * 512, :].rearrange("(k p) c -> p k c", p=128)
                    ins = sy.dma_start(out=wo_rg[J % 2].rearrange("p (k c) -> p k c", k=4), in_=src)
                    sem.inc(ins, "woa" if J % 2 == 0 else "wob2", 16)
                for j in [1]:
                    J = 8 * g + j
                    if J < 2:
                        sem.wait(sy, "pa", NG)
                    else:
                        ref = J - 2
                        if ref % 4 == 3:
                            sem.wait(sy, "omm", 2 * (ref // 4) + 2)
                        else:
                            sem.wait(sy, "wob", 3 * (ref // 4) + ref % 4 + 1)
                    jj = j % 4
                    src = wo_bf[jj * 512:(jj + 1) * 512, :].rearrange("(k p) c -> p k c", p=128)
                    ins = sy.dma_start(out=wo_rg[J % 2].rearrange("p (k c) -> p k c", k=4), in_=src)
                    sem.inc(ins, "woa" if J % 2 == 0 else "wob2", 16)
                for j in [2]:
                    J = 8 * g + j
                    if J < 2:
                        sem.wait(sy, "pa", NG)
                    else:
                        ref = J - 2
                        if ref % 4 == 3:
                            sem.wait(sy, "omm", 2 * (ref // 4) + 2)
                        else:
                            sem.wait(sy, "wob", 3 * (ref // 4) + ref % 4 + 1)
                    jj = j % 4
                    src = wo_bf[jj * 512:(jj + 1) * 512, :].rearrange("(k p) c -> p k c", p=128)
                    ins = sy.dma_start(out=wo_rg[J % 2].rearrange("p (k c) -> p k c", k=4), in_=src)
                    sem.inc(ins, "woa" if J % 2 == 0 else "wob2", 16)
                for j in [3]:
                    J = 8 * g + j
                    if J < 2:
                        sem.wait(sy, "pa", NG)
                    else:
                        ref = J - 2
                        if ref % 4 == 3:
                            sem.wait(sy, "omm", 2 * (ref // 4) + 2)
                        else:
                            sem.wait(sy, "wob", 3 * (ref // 4) + ref % 4 + 1)
                    jj = j % 4
                    src = wo_bf[jj * 512:(jj + 1) * 512, :].rearrange("(k p) c -> p k c", p=128)
                    ins = sy.dma_start(out=wo_rg[J % 2].rearrange("p (k c) -> p k c", k=4), in_=src)
                    sem.inc(ins, "woa" if J % 2 == 0 else "wob2", 16)
                for ti in range(2, TPG):
                    t = 4 * g + ti
                    sem.wait(sy, "ev2", 8 * ((t - 2) // 4) + [1, 2, 4, 6][(t - 2) % 4])
                    ins = sy.dma_start(out=Xb[t % 2], in_=X_d[t * 128:(t + 1) * 128, :])
                    sem.inc(ins, "x2la" if t % 2 == 0 else "x2lb", 16)
                for j in [4]:
                    J = 8 * g + j
                    if J < 2:
                        sem.wait(sy, "pa", NG)
                    else:
                        ref = J - 2
                        if ref % 4 == 3:
                            sem.wait(sy, "omm", 2 * (ref // 4) + 2)
                        else:
                            sem.wait(sy, "wob", 3 * (ref // 4) + ref % 4 + 1)
                    jj = j % 4
                    src = wo_bf[jj * 512:(jj + 1) * 512, :].rearrange("(k p) c -> p k c", p=128)
                    ins = sy.dma_start(out=wo_rg[J % 2].rearrange("p (k c) -> p k c", k=4), in_=src)
                    sem.inc(ins, "woa" if J % 2 == 0 else "wob2", 16)
                for j in [5]:
                    J = 8 * g + j
                    if J < 2:
                        sem.wait(sy, "pa", NG)
                    else:
                        ref = J - 2
                        if ref % 4 == 3:
                            sem.wait(sy, "omm", 2 * (ref // 4) + 2)
                        else:
                            sem.wait(sy, "wob", 3 * (ref // 4) + ref % 4 + 1)
                    jj = j % 4
                    src = wo_bf[jj * 512:(jj + 1) * 512, :].rearrange("(k p) c -> p k c", p=128)
                    ins = sy.dma_start(out=wo_rg[J % 2].rearrange("p (k c) -> p k c", k=4), in_=src)
                    sem.inc(ins, "woa" if J % 2 == 0 else "wob2", 16)
                for j in [6]:
                    J = 8 * g + j
                    if J < 2:
                        sem.wait(sy, "pa", NG)
                    else:
                        ref = J - 2
                        if ref % 4 == 3:
                            sem.wait(sy, "omm", 2 * (ref // 4) + 2)
                        else:
                            sem.wait(sy, "wob", 3 * (ref // 4) + ref % 4 + 1)
                    jj = j % 4
                    src = wo_bf[jj * 512:(jj + 1) * 512, :].rearrange("(k p) c -> p k c", p=128)
                    ins = sy.dma_start(out=wo_rg[J % 2].rearrange("p (k c) -> p k c", k=4), in_=src)
                    sem.inc(ins, "woa" if J % 2 == 0 else "wob2", 16)
                for j in [7]:
                    J = 8 * g + j
                    if J < 2:
                        sem.wait(sy, "pa", NG)
                    else:
                        ref = J - 2
                        if ref % 4 == 3:
                            sem.wait(sy, "omm", 2 * (ref // 4) + 2)
                        else:
                            sem.wait(sy, "wob", 3 * (ref // 4) + ref % 4 + 1)
                    jj = j % 4
                    src = wo_bf[jj * 512:(jj + 1) * 512, :].rearrange("(k p) c -> p k c", p=128)
                    ins = sy.dma_start(out=wo_rg[J % 2].rearrange("p (k c) -> p k c", k=4), in_=src)
                    sem.inc(ins, "woa" if J % 2 == 0 else "wob2", 16)
            sem.wait(sy, "c_g", CG)
            sem.wait(sy, "c_u", CU)

            def p3_fmload_hi(g):
                if g == 0:
                    last_even = NG - 2 if NG % 2 == 0 else NG - 1
                    sem.wait(sy, "qmm", 16 * (last_even + 1))
                else:
                    sem.wait(sy, "umm", 16 * g)
                sem.wait(sy, "sp2a", 32 * (NT // 2))
                sem.wait(sy, "sp2b", 32 * (NT // 2))
                for c in range(DC // 2, DC):
                    ins = sy.dma_start_transpose(
                        x2n_fm[:, c * GT:(c + 1) * GT],
                        x2n_sp[g * GT:(g + 1) * GT, c * 128:(c + 1) * 128])
                    sem.inc(ins, "fm3", 16)

            for g in range(NG):
                for c in range(16):
                    for s in range(2):
                        j = 32 * g + 2 * c + s
                        if j >= 3:
                            jc = j - 3
                            if jc % 2 == 0:
                                sem.wait(sy, "gmm", jc // 2 + 1)
                            else:
                                sem.wait(sy, "umm", jc // 2 + 1)
                        elif g == 0 and j == 0:
                            sem.wait(sy, "ev2", 2 * NT)
                            sem.wait(sy, "sp2a", 32 * (NT // 2))
                            sem.wait(sy, "sp2b", 32 * (NT // 2))
                        wsrc = (wg_bf if s == 0 else wu_bf)[:, c * 512:(c + 1) * 512]
                        src = wsrc.rearrange("(k p) c2 -> p k c2", p=128)
                        ins = sy.dma_start(out=wgu[j % 3].rearrange("p (k c2) -> p k c2", k=DC), in_=src)
                        sem.inc(ins, "wgu" + str(j % 3), 16)
                if g == 0:
                    sem.wait(sy, "c_d", CD)
                for h in range(2):
                    if h == 1 and g + 1 < NG:
                        p3_fmload_hi(g + 1)
                    for j in range(16):
                        J = 32 * g + 16 * h + j
                        if J >= 2:
                            ref = J - 2
                            if ref % 16 == 15:
                                sem.wait(sy, "dmm", ref // 16 + 1)
                            else:
                                sem.wait(sy, "dmk", 15 * (ref // 16) + ref % 16 + 1)
                        src = wd_bf[j * 512:(j + 1) * 512, h * 1024:(h + 1) * 1024]
                        src = src.rearrange("(k p) c -> p k c", p=128)
                        ins = sy.dma_start(out=wd_rg[J % 2].rearrange("p (k c) -> p k c", k=4), in_=src)
                        sem.inc(ins, "wkva" if J % 2 == 0 else "wkvb", 16)
                    for ee in range(2):
                        if g >= 1 and h == 0 and ee == 0:
                            for nm in ["o3t00", "o3t01", "o3t10", "o3t11"]:
                                sem.wait(sy, nm, 32 * g)
                        sem.wait(sy, "o3ev", 4 * g + 2 * h + ee + 1)
                        ins = sy.dma_start(
                            out=o3_sp[h * 1024 + ee * 512:h * 1024 + (ee + 1) * 512, :]
                            .rearrange("(c p) t -> p c t", p=128),
                            in_=o3bf.rearrange("p (c t) -> p c t", c=4))
                        sem.inc(ins, "o3sp", 16)
                def x2r_load(t):
                    if t < 2:
                        sem.wait(sy, "ev2", 2 * NT)
                        sem.wait(sy, "sp2a", 32 * (NT // 2))
                        sem.wait(sy, "sp2b", 32 * (NT // 2))
                    else:
                        sem.wait(sy, "fin", t - 1)
                        sem.wait(sy, "outwa" if t % 2 == 0 else "outwb",
                                 16 * (t // 2))
                    ins = sy.dma_start(out=x2r[t % 2], in_=x2_sp[t * 128:(t + 1) * 128, :])
                    sem.inc(ins, "x2ra" if t % 2 == 0 else "x2rb", 16)

                def out_write(t):
                    sem.wait(sy, "fin", t + 1)
                    ins = sy.dma_start(out=OUT_d[t * 128:(t + 1) * 128, :], in_=x2r[t % 2])
                    sem.inc(ins, "outwa" if t % 2 == 0 else "outwb", 16)

                x2r_load(4 * g)
                x2r_load(4 * g + 1)
                out_write(4 * g)
                x2r_load(4 * g + 2)
                out_write(4 * g + 1)
                x2r_load(4 * g + 3)
                out_write(4 * g + 2)
                out_write(4 * g + 3)
            if DEBUG_DUMP:
                ins = sy.dma_start(out=dbg_xn, in_=xn_sp)
                sem.inc(ins, "csld", 16)
                ins = sy.dma_start(out=dbg_x2, in_=x2_sp)
                sem.inc(ins, "csld", 16)
                ins = sy.dma_start(out=dbg_x2n, in_=x2n_sp)
                sem.inc(ins, "csld", 16)
                ins = sy.dma_start(out=dbg_o3, in_=o3_sp)
                sem.inc(ins, "csld", 16)
                ins = sy.dma_start(out=dbg_ar, in_=ar8[:])
                sem.inc(ins, "csld", 16)
            for n2 in ["x1a", "x1b", "x2la", "x2lb", "sp1a", "sp1b", "sp2a", "sp2b",
                       "wkva", "wkvb", "wqa", "wqb", "woa", "wob2",
                       "wgu0", "wgu1", "wgu2", "x2ra", "x2rb", "o3sp", "csld",
                       "outwa", "outwb"]:
                sem.wait(sy, n2, sem.total[n2])

        @block.scalar
        def _(sc):
            sem.wait(sc, "csld", 32)
            def sqrt_pair(t):
                sem.wait(sc, "vstat", 4 * t + 1)
                ins = sc.activation(stats[:, t:t + 1], scrS, AF.Identity, scale=INV_D)
                sem.inc(ins, "sstd", 1)
                sem.wait(sc, "vstat", 4 * t + 2)
                sc.activation(tl_s, scrS, AF.Ln, bias=EPS, scale=INV_D)
                ins = sc.activation(stats[:, NT + t:NT + t + 1], tl_s, AF.Exp, scale=-0.5)
                sem.inc(ins, "sstd", 1)
                sem.wait(sc, "vstat", 4 * t + 3)
                ins = sc.activation(mu2s, scrS, AF.Identity, scale=INV_D)
                sem.inc(ins, "sstd", 1)
                sem.wait(sc, "vstat", 4 * t + 4)
                sc.activation(tl_s, scrS, AF.Ln, bias=EPS, scale=INV_D)
                ins = sc.activation(r2s, tl_s, AF.Exp, scale=-0.5)
                sem.inc(ins, "sstd", 1)

            def kv_evac(gg, m):
                sem.wait(sc, "kvmm", 32 * gg + m + 1)
                if m == 0:
                    sem.wait(sc, "pa", gg)
                dst = K_fm[:, m * GT:(m + 1) * GT] if m < DC else \
                    V_fm[:, (m - DC) * GT:(m - DC + 1) * GT]
                ins = sc.activation(dst, bank(m % 2), AF.Identity,
                                    bias=bkv_fm[:, m:m + 1], scale=1.0)
                sem.inc(ins, "kvev", 1)

            for g in range(NG + 1):
                for ti in range(TPG):
                    if g < NG:
                        sqrt_pair(4 * g + ti)
                    if g < NG and ti == 3:
                        sem.wait(sc, "sp1a", 16 * (2 * g + 2))
                        sem.wait(sc, "sp1b", 16 * (2 * g + 2))
                        sem.wait(sc, "kvmm", 32 * (g - 1))
                        for c in range(DC // 2):
                            ins = sc.dma_start_transpose(
                                Xn_fm[g % 2][:, c * GT:(c + 1) * GT],
                                xn_sp[g * GT:(g + 1) * GT, c * 128:(c + 1) * 128])
                            sem.inc(ins, "fm1a" if g % 2 == 0 else "fm1b", 16)
                    if g >= 1:
                        for m in range(8 * ti, 8 * ti + 8):
                            kv_evac(g - 1, m)
                if g >= 1:
                    gg = g - 1
                    sem.wait(sc, "nmm", 8 * gg + 8)
                    ins = sc.copy(nk_sb, ps[0:1, 2 * 512:2 * 512 + GT])
                    sem.inc(ins, "rnv", 1)
                    sem.wait(sc, "rnv", 3 * gg + 2)
                    sc.activation(rn_sb, rn_sb, AF.Ln)
                    ins = sc.activation(rn_sb, rn_sb, AF.Exp, scale=-0.5)
                    sem.inc(ins, "rnv", 1)
                if g < NG:
                    sem.wait(sc, "sqv", 8 * (g - 1) + 1)
                    ins = sc.dma_start(out=cos_sb.rearrange("p (c t) -> p c t", c=4),
                                       in_=cos_d[:, g * GT:(g + 1) * GT]
                                       .rearrange("(c p) t -> p c t", p=128))
                    sem.inc(ins, "cssem", 16)
                    ins = sc.dma_start(out=sin_sb.rearrange("p (c t) -> p c t", c=4),
                                       in_=sin_d[:, g * GT:(g + 1) * GT]
                                       .rearrange("(c p) t -> p c t", p=128))
                    sem.inc(ins, "cssem", 16)

            def p2_fmload(g):
                sem.wait(sc, "qmm", 16 * (g - 1))
                for c in range(DC // 2):
                    ins = sc.dma_start_transpose(
                        Xn_fm[g % 2][:, c * GT:(c + 1) * GT],
                        xn_sp[g * GT:(g + 1) * GT, c * 128:(c + 1) * 128])
                    sem.inc(ins, "fm1a" if g % 2 == 0 else "fm1b", 16)

            sem.wait(sc, "sp1a", 16 * (NT // 2))
            sem.wait(sc, "sp1b", 16 * (NT // 2))
            p2_fmload(0)
            for t in range(NT):
                sc.activation(stats[:, t:t + 1], stats[:, t:t + 1], AF.Identity,
                              scale=stats[:, NT + t:NT + t + 1])
                ins = sc.activation(stats[:, NT + t:NT + t + 1],
                                    stats[:, NT + t:NT + t + 1], AF.Identity, bias=1.0)
                if t == NT - 1:
                    sem.inc(ins, "sc3", 1)
            for g in range(NG):
                if g == 0:
                    sem.wait(sc, "afin", 1)
                for m in range(DC):
                    sem.wait(sc, "qmm", 16 * g + m + 1)
                    ins = sc.activation(Y_fm[:, m * GT:(m + 1) * GT], bank(m % 2), AF.Identity,
                                        bias=Abq[:, m:m + 1], scale=A_fm[:, m:m + 1])
                    sem.inc(ins, "yev", 1)
                if g + 1 < NG:
                    p2_fmload(g + 1)
                for ti in range(TPG):
                    t = 4 * g + ti
                    sem.wait(sc, "v3", 2 * t + 1)
                    ins = sc.activation(mu2s, scrS, AF.Identity, scale=INV_D)
                    sem.inc(ins, "sc3", 1)
                    sem.wait(sc, "v3", 2 * t + 2)
                    sc.activation(tl_s, scrS, AF.Ln, bias=EPS, scale=INV_D)
                    ins = sc.activation(r2s, tl_s, AF.Exp, scale=-0.5)
                    sem.inc(ins, "sc3", 1)
                    sem.wait(sc, "ev2", 8 * (t // 4) + [3, 5, 7, 8][t % 4])
                    s2n = "sp2a" if t % 2 == 0 else "sp2b"
                    ins = sc.dma_start(out=x2_sp[t * 128:(t + 1) * 128, :], in_=x2b[t % 2])
                    sem.inc(ins, s2n, 16)
                    ins = sc.dma_start(out=x2n_sp[t * 128:(t + 1) * 128, :], in_=x2n_tm[t % 2])
                    sem.inc(ins, s2n, 16)

            def p3_fmload(g):
                sem.wait(sc, "umm", 16 * g)
                for c in range(DC if g == 0 else DC // 2):
                    ins = sc.dma_start_transpose(
                        x2n_fm[:, c * GT:(c + 1) * GT],
                        x2n_sp[g * GT:(g + 1) * GT, c * 128:(c + 1) * 128])
                    sem.inc(ins, "fm3", 16)

            sem.wait(sc, "sp2a", 32 * (NT // 2))
            sem.wait(sc, "sp2b", 32 * (NT // 2))
            p3_fmload(0)
            for g in range(NG):
                for c in range(16):
                    C = 16 * g + c
                    sem.wait(sc, "gmm", C + 1)
                    sem.wait(sc, "spar", C)
                    ins = sc.activation(sg_sb, bank(0, 4), AF.Silu)
                    sem.inc(ins, "silu", 1)
                    sem.wait(sc, "hdn", C + 1)
                    ins = sc.activation(ah, h_f, AF.Abs)
                    sem.inc(ins, "silu", 1)
                if g + 1 < NG:
                    p3_fmload(g + 1)
                for h in range(2):
                    sem.wait(sc, "dmm", 2 * g + h + 1)
                    for ee in range(2):
                        sem.wait(sc, "o3sp", 16 * (4 * g + 2 * h + ee))
                        ins = sc.copy(o3bf, ps[:, ee * 2048:(ee + 1) * 2048])
                        sem.inc(ins, "o3ev", 1)
                sem.wait(sc, "o3sp", 16 * (4 * g + 4))
                for ti in range(TPG):
                    t = 4 * g + ti
                    if t < 2 and g == 0:
                        sem.wait(sc, "ev2", 2 * NT)
                    elif t >= 2:
                        sem.wait(sc, "fin", t - 1)
                    for h in range(2):
                        ins = sc.dma_start_transpose(
                            o3tm[t % 2][:, h * 1024:(h + 1) * 1024],
                            o3_sp[h * 1024:(h + 1) * 1024, ti * 128:(ti + 1) * 128])
                        sem.inc(ins, "o3t" + str(t % 2) + str(h), 16)
            for n2 in ["fm1a", "fm1b", "fm3", "cssem", "o3t00", "o3t01", "o3t10", "o3t11"]:
                sem.wait(sc, n2, sem.total[n2])

        @block.tensor
        def _(te):
            for g in range(NG):
                sem.wait(te, "fm1a" if g % 2 == 0 else "fm1b", 16 * 16 * (g // 2 + 1))
                for m in range(32):
                    J = 16 * g + m // 2
                    sem.wait(te, "wkva" if J % 2 == 0 else "wkvb", 16 * (J // 2 + 1))
                    sem.wait(te, "kvev", 32 * g + m - 1)
                    msel = m % 2
                    ins = None
                    for k in range(DC):
                        lhsT = wkv_rg[J % 2][:, k * 256 + msel * 128:k * 256 + msel * 128 + 128]
                        rhs = Xn_fm[g % 2][:, k * GT:(k + 1) * GT]
                        ins = te.matmul(bank(m % 2), lhsT, rhs,
                                        start=(k == 0), stop=(k == DC - 1))
                    sem.inc(ins, "kvmm", 1)
                for tn in range(2):
                    pb = 2 + tn
                    if tn == 0:
                        sem.wait(te, "rnv", 3 * (g - 1) + 1)
                    else:
                        sem.wait(te, "rnv", 3 * (g - 1) + 2)
                    for q in range(4):
                        sem.wait(te, "sqv", 8 * g + 4 * tn + q + 1)
                        ins = None
                        for c in range(4):
                            ins = te.matmul(ps[0:1, pb * 512:pb * 512 + GT],
                                            onesb, sqnb[q % 2][:, c * GT:(c + 1) * GT],
                                            start=(q == 0 and c == 0),
                                            stop=(q == 3 and c == 3))
                        sem.inc(ins, "nmm", 1)
                sem.wait(te, "rnv", 3 * g + 3)
                sem.wait(te, "pa", g)
                ins = te.matmul(bank(4), ones1, rn_sb, start=True, stop=True)
                sem.inc(ins, "rnbc", 1)

            sem.wait(te, "pa", NG)
            for g in range(NG):
                sem.wait(te, "fm1a" if g % 2 == 0 else "fm1b",
                         16 * 16 * (NG // 2 + g // 2 + 1))
                for m in range(DC):
                    J = 8 * g + m // 2
                    if m == 0 and g >= 1:
                        sem.wait(te, "ev2", 8 * (g - 1) + 4)
                    sem.wait(te, "wqa" if J % 2 == 0 else "wqb", 16 * (J // 2 + 1))
                    sem.wait(te, "yev", 16 * g + m - 1)
                    msel = m % 2
                    ins = None
                    for k in range(DC):
                        lhsT = wq_rg[J % 2][:, k * 256 + msel * 128:k * 256 + msel * 128 + 128]
                        rhs = Xn_fm[g % 2][:, k * GT:(k + 1) * GT]
                        ins = te.matmul(bank(m % 2), lhsT, rhs,
                                        start=(k == 0), stop=(k == DC - 1))
                    sem.inc(ins, "qmm", 1)
                sem.wait(te, "yev", 16 * (g + 1))
                for mtp in range(2):
                    last = {}
                    for j in range(4):
                        J = 8 * g + 4 * mtp + j
                        sem.wait(te, "woa" if J % 2 == 0 else "wob2", 16 * (J // 2 + 1))
                        for kk in range(4):
                            k = 4 * j + kk
                            for q in range(2):
                                mt = 2 * mtp + q
                                t = 4 * g + mt
                                if j == 0 and kk == 0 and t >= 2:
                                    sem.wait(te, "ev2", 8 * ((t - 2) // 4) + [1, 2, 4, 6][(t - 2) % 4])
                                pset = (mt % 2) * 4
                                lhsT = Y_fm[:, k * GT + mt * 128:k * GT + mt * 128 + 128]
                                for n in range(4):
                                    last[mt] = te.matmul(
                                        ps[:, (pset + n) * 512:(pset + n + 1) * 512],
                                        lhsT, wo_rg[J % 2][:, kk * D + n * 512:kk * D + (n + 1) * 512],
                                        start=(k == 0), stop=(k == DC - 1),
                                        skip_group_check=True)
                        if j < 3:
                            sem.inc(last[2 * mtp + 1], "wob", 1)
                    for q in range(2):
                        sem.inc(last[2 * mtp + q], "omm", 1)

            sem.wait(te, "ev2", 2 * NT)
            for g in range(NG):
                sem.wait(te, "fm3", 16 * (16 * g + 16))
                sem.wait(te, "o3ev", 4 * g)
                for c in range(16):
                    C = 16 * g + c
                    j0 = 2 * C
                    sem.wait(te, "wgu" + str(j0 % 3), 16 * (j0 // 3 + 1))
                    sem.wait(te, "silu", 2 * C - 1)
                    ins = None
                    for m in range(4):
                        for k in range(DC):
                            lhsT = wgu[(2 * C) % 3][:, k * GT + m * 128:k * GT + m * 128 + 128]
                            ins = te.matmul(bank(m), lhsT, x2n_fm[:, k * GT:(k + 1) * GT],
                                            start=(k == 0), stop=(k == DC - 1))
                    sem.inc(ins, "gmm", 1)
                    j1 = 2 * C + 1
                    sem.wait(te, "wgu" + str(j1 % 3), 16 * (j1 // 3 + 1))
                    sem.wait(te, "hdn", C)
                    ins = None
                    for m in range(4):
                        for k in range(DC):
                            lhsT = wgu[(2 * C + 1) % 3][:, k * GT + m * 128:k * GT + m * 128 + 128]
                            ins = te.matmul(bank(4 + m), lhsT, x2n_fm[:, k * GT:(k + 1) * GT],
                                            start=(k == 0), stop=(k == DC - 1))
                    sem.inc(ins, "umm", 1)
                sem.wait(te, "spar", 16 * g + 16)
                for h in range(2):
                    if not (g == 0 and h == 0):
                        sem.wait(te, "o3ev", 4 * g + 2 * h)
                    ins = None
                    for j in range(16):
                        J = 32 * g + 16 * h + j
                        sem.wait(te, "wkva" if J % 2 == 0 else "wkvb",
                                 16 * (8 * NG + J // 2 + 1))
                        for kk in range(4):
                            k = 4 * j + kk
                            rhs = h_s[:, k * GT:(k + 1) * GT]
                            for m in range(8):
                                lhsT = wd_rg[J % 2][:, kk * 1024 + m * 128:kk * 1024 + (m + 1) * 128]
                                ins = te.matmul(bank(m), lhsT, rhs,
                                                start=(k == 0), stop=(k == 63),
                                                skip_group_check=True)
                        if j < 15:
                            sem.inc(ins, "dmk", 1)
                    sem.inc(ins, "dmm", 1)

        @block.vector
        def _(ve):
            ve.memset(onesc, 1.0)
            ve.memset(onesb, 1.0)
            ve.memset(ones1, 1.0)
            for ci, cvv in enumerate(CONSTS):
                ve.memset(cstv[:, ci:ci + 1], cvv)
            sem.wait(ve, "csld", 32)
            sem.wait(ve, "cgb", 32)
            for g in range(NG + 1):
                if g < NG:
                    for ti in range(TPG):
                        t = 4 * g + ti
                        sem.wait(ve, "x1a" if t % 2 == 0 else "x1b", 16 * (t // 2 + 1))
                        X = Xb[t % 2]
                        ins = ve.reduce_sum(scrS, X, axis=AX.X)
                        sem.inc(ins, "vstat", 1)
                        sem.wait(ve, "sstd", 4 * t + 1)
                        ins = ve.tensor_scalar(xc, X, stats[:, t:t + 1], None, ALU.subtract)
                        sem.inc(ins, "xf1", 1)
                        ins = ve.scalar_tensor_tensor(sqb, xc, 1.0, xc, ALU.mult, ALU.mult,
                                                      accum_out=scrS)
                        sem.inc(ins, "vstat", 1)
                        sem.wait(ve, "sstd", 4 * t + 2)
                        ins = ve.tensor_scalar(xc, xc, stats[:, NT + t:NT + t + 1], None, ALU.mult)
                        ins = ve.reduce_sum(scrS, xc, axis=AX.X)
                        sem.inc(ins, "vstat", 1)
                        sem.wait(ve, "sstd", 4 * t + 3)
                        ve.tensor_scalar(xc, xc, mu2s, None, ALU.subtract)
                        ins = ve.scalar_tensor_tensor(sqb, xc, 1.0, xc, ALU.mult, ALU.mult,
                                                      accum_out=scrS)
                        sem.inc(ins, "vstat", 1)
                        sem.wait(ve, "sstd", 4 * t + 4)
                        ve.scalar_tensor_tensor(xc, xc, r2s, ga_rep, ALU.mult, ALU.mult)
                        if t >= 2:
                            sem.wait(ve, "sp1a" if t % 2 == 0 else "sp1b", 16 * (t // 2))
                        ins = ve.tensor_tensor(xn_tm[t % 2], xc, be_rep, ALU.add)
                        sem.inc(ins, "ln", 1)
                if g >= 1:
                    gg = g - 1
                    sem.wait(ve, "nmm", 8 * gg)
                    sem.wait(ve, "cssem", 32 * (gg + 1))
                    for tn in range(2):
                        T_ = K_fm if tn == 0 else V_fm
                        sem.wait(ve, "kvev", 32 * gg + 16 * tn + 8)
                        ev = T_[:, 0:4 * GT]
                        od = T_[:, 4 * GT:8 * GT]
                        ve.tensor_tensor(t1, ev, cos_sb, ALU.mult)
                        ve.tensor_tensor(t2, od, sin_sb, ALU.mult)
                        ve.tensor_tensor(t1, t1, t2, ALU.subtract)
                        ve.tensor_tensor(t2, od, cos_sb, ALU.mult)
                        ve.tensor_tensor(od, ev, sin_sb, ALU.mult)
                        ve.tensor_tensor(od, od, t2, ALU.add)
                        ve.tensor_copy(ev, t1)
                    for tn in range(2):
                        T_ = K_fm if tn == 0 else V_fm
                        sem.wait(ve, "kvev", 32 * gg + 16 + 16 * tn)
                        for q in range(4):
                            qb = 8 * gg + 4 * tn + q
                            sem.wait(ve, "nmm", qb - 1)
                            sl = T_[:, q * 4 * GT:(q + 1) * 4 * GT]
                            ins = ve.tensor_tensor(sqnb[q % 2], sl, sl, ALU.mult)
                            sem.inc(ins, "sqv", 1)
                    sem.wait(ve, "rnv", 3 * gg + 1)
                    ins = ve.tensor_tensor(rn_sb, nk_sb, ps[0:1, 3 * 512:3 * 512 + GT], ALU.mult)
                    sem.inc(ins, "rnv", 1)
                    sem.wait(ve, "rnbc", gg + 1)
                    ve.tensor_copy(rnbc, bank(4))
                    ve.tensor_tensor(K_fm, K_fm, V_fm, ALU.mult)
                    rb = rnbc.rearrange("p (c t) -> p c t", c=1).broadcast_to([128, DC, GT])
                    ve.tensor_tensor(K_fm.rearrange("p (c t) -> p c t", c=DC),
                                     K_fm.rearrange("p (c t) -> p c t", c=DC), rb, ALU.mult)
                    ins = ve.reduce_sum(Agrp[:, 16 * gg:16 * gg + 16],
                                        K_fm.rearrange("p (c t) -> p c t", c=DC), axis=AX.X)
                    sem.inc(ins, "pa", 1)
            ins = ve.tensor_copy(xc[:, 0:16], Agrp[:, 0:16])
            sem.inc(ins, "vself", 1)
            vs = 1
            sem.wait(ve, "vself", vs)
            idxs = list(range(NG))
            while len(idxs) > 1:
                nxt = []
                ins = None
                for i in range(0, len(idxs) - 1, 2):
                    a_, b_ = idxs[i], idxs[i + 1]
                    ins = ve.tensor_tensor(Agrp[:, 16 * a_:16 * a_ + 16],
                                           Agrp[:, 16 * a_:16 * a_ + 16],
                                           Agrp[:, 16 * b_:16 * b_ + 16], ALU.add)
                    nxt.append(a_)
                if len(idxs) % 2 == 1:
                    nxt.append(idxs[-1])
                sem.inc(ins, "vself", 1)
                vs += 1
                sem.wait(ve, "vself", vs)
                idxs = nxt
            ins = ve.tensor_scalar(A_fm, Agrp[:, 16 * idxs[0]:16 * idxs[0] + 16],
                                   0.5, 0.5, ALU.mult, ALU.add)
            sem.inc(ins, "vself", 1)
            vs += 1
            sem.wait(ve, "vself", vs)
            ins = ve.tensor_tensor(Abq, A_fm, bq_fm, ALU.mult)
            sem.inc(ins, "afin", 1)

            sem.wait(ve, "cgb", 48)
            sem.wait(ve, "sc3", 1)
            EV_E = [1, 2, 4, 6]
            EV_L = [3, 5, 7, 8]

            def p2_evac(g, mt):
                t = 4 * g + mt
                sem.wait(ve, "omm", t + 1)
                sem.wait(ve, "x2la" if t % 2 == 0 else "x2lb", 16 * (t // 2 + 1))
                X = Xb[t % 2]
                x2 = x2b[t % 2]
                pset = (mt % 2) * 4
                o2 = ps[:, pset * 512:(pset + 4) * 512]
                if t >= 2:
                    sem.wait(ve, "sp2a" if t % 2 == 0 else "sp2b", 32 * (t // 2))
                ve.tensor_tensor(xc, o2, bo_rep, ALU.add)
                ve.tensor_scalar(x2, X, stats[:, NT + t:NT + t + 1],
                                 stats[:, t:t + 1], ALU.mult, ALU.subtract)
                ins = ve.tensor_tensor(x2, x2, xc, ALU.add)
                sem.inc(ins, "ev2", 1)

            def p2_ln3(g, mt):
                t = 4 * g + mt
                x2 = x2b[t % 2]
                ins = ve.reduce_sum(scrS, x2, axis=AX.X)
                sem.inc(ins, "v3", 1)
                sem.wait(ve, "sc3", 1 + 2 * t + 1)
                ve.tensor_scalar(xc, x2, mu2s, None, ALU.subtract)
                ins = ve.scalar_tensor_tensor(sqb, xc, 1.0, xc, ALU.mult, ALU.mult,
                                              accum_out=scrS)
                sem.inc(ins, "v3", 1)
                sem.wait(ve, "sc3", 1 + 2 * t + 2)
                if t < 2:
                    sem.wait(ve, "sp1a", 16 * (NT // 2))
                    sem.wait(ve, "sp1b", 16 * (NT // 2))
                ins = ve.tensor_scalar(x2n_tm[t % 2], xc, r2s, None, ALU.mult)
                sem.inc(ins, "ev2", 1)

            for g in range(NG):
                p2_evac(g, 0)
                p2_evac(g, 1)
                p2_ln3(g, 0)
                p2_evac(g, 2)
                p2_ln3(g, 1)
                p2_evac(g, 3)
                p2_ln3(g, 2)
                p2_ln3(g, 3)

            for g in range(NG):
                for c in range(16):
                    C = 16 * g + c
                    sem.wait(ve, "silu", 2 * C + 1)
                    sem.wait(ve, "umm", C + 1)
                    ins = ve.tensor_tensor(h_f, sg_sb, bank(4, 4), ALU.mult)
                    sem.inc(ins, "hdn", 1)
                    sem.wait(ve, "silu", 2 * C + 2)
                    qv = [ah[:, i * GT:(i + 1) * GT] for i in range(4)]
                    ve.tensor_tensor(dcmp[0], qv[0], qv[1], ALU.is_ge)
                    ve.tensor_tensor(dcmp[1], qv[0], qv[2], ALU.is_ge)
                    ve.tensor_tensor(dcmp[2], qv[0], qv[3], ALU.is_ge)
                    ve.tensor_tensor(dcmp[3], qv[1], qv[2], ALU.is_ge)
                    ve.tensor_tensor(dcmp[4], qv[1], qv[3], ALU.is_ge)
                    ve.tensor_tensor(dcmp[5], qv[2], qv[3], ALU.is_ge)
                    ve.tensor_tensor(srank[0], dcmp[0], dcmp[1], ALU.add)
                    ve.tensor_tensor(srank[0], srank[0], dcmp[2], ALU.add)
                    ve.tensor_tensor(srank[1], dcmp[3], dcmp[4], ALU.add)
                    ve.tensor_tensor(srank[1], srank[1], dcmp[0], ALU.subtract)
                    ve.tensor_tensor(srank[2], dcmp[1], dcmp[3], ALU.add)
                    ve.tensor_tensor(srank[2], srank[2], dcmp[5], ALU.subtract)
                    ve.tensor_tensor(srank[3], dcmp[2], dcmp[4], ALU.add)
                    ve.tensor_tensor(srank[3], srank[3], dcmp[5], ALU.add)
                    ve.tensor_scalar(srank[0], srank[0], 2.0, None, ALU.is_ge)
                    ve.tensor_scalar(srank[1], srank[1], 1.0, None, ALU.is_ge)
                    ve.tensor_scalar(srank[2], srank[2], 0.0, None, ALU.is_le)
                    ve.tensor_scalar(srank[3], srank[3], 1.0, None, ALU.is_le)
                    if c == 0:
                        sem.wait(ve, "dmm", 2 * g)
                    ins = None
                    for i2 in range(4):
                        ins = ve.tensor_tensor(h_s[:, (4 * c + i2) * GT:(4 * c + i2 + 1) * GT],
                                               h_f[:, i2 * GT:(i2 + 1) * GT], srank[i2], ALU.mult)
                    sem.inc(ins, "spar", 1)
                for ti in range(TPG):
                    t = 4 * g + ti
                    sem.wait(ve, "o3t" + str(t % 2) + "0", 16 * (2 * g + ti // 2 + 1))
                    sem.wait(ve, "o3t" + str(t % 2) + "1", 16 * (2 * g + ti // 2 + 1))
                    sem.wait(ve, "x2ra" if t % 2 == 0 else "x2rb", 16 * (t // 2 + 1))
                    if t >= 2:
                        sem.wait(ve, "outwa" if t % 2 == 0 else "outwb", 16 * (t // 2))
                    ins = ve.tensor_tensor(x2r[t % 2], x2r[t % 2], o3tm[t % 2], ALU.add)
                    sem.inc(ins, "fin", 1)

        sem.check()
    return nc


def _host_prep(L0_attn_X, L0_attn_W_in, L0_attn_b_in, L0_attn_W_out, L0_attn_b_out,
               L0_attn_gamma, L0_attn_beta, L0_ffn_w_gate, L0_ffn_w_up, L0_ffn_w_down):
    f32 = np.float32
    W_in = np.asarray(L0_attn_W_in, f32)
    b_in = np.asarray(L0_attn_b_in, f32)

    rho = np.concatenate([np.arange(0, 1024, 2), np.arange(1, 1024, 2),
                          np.arange(1024, 2048)])
    Wq = np.ascontiguousarray(W_in[:, 0:D][:, rho])
    Wk = W_in[:, D:2 * D][:, rho]
    Wv = W_in[:, 2 * D:3 * D][:, rho]
    Wkv = np.ascontiguousarray(np.concatenate([Wk, Wv], axis=1))
    bq = b_in[0:D][rho]
    bkv = np.concatenate([b_in[D:2 * D][rho], b_in[2 * D:3 * D][rho]])

    b_idx = np.arange(16)[:, None, None]
    k_idx = np.arange(4)[None, :, None]
    r_idx = np.arange(128)[None, None, :]
    P = (4 * (128 * b_idx + r_idx) + k_idx).reshape(-1)
    Wg = np.ascontiguousarray(np.asarray(L0_ffn_w_gate, f32)[:, P])
    Wu = np.ascontiguousarray(np.asarray(L0_ffn_w_up, f32)[:, P])
    Wd = np.ascontiguousarray(np.asarray(L0_ffn_w_down, f32)[P, :])

    bq_fm = np.ascontiguousarray(bq.reshape(DC, 128).T)
    bkv_fm = np.ascontiguousarray(bkv.reshape(2 * DC, 128).T)
    bo_rep = np.ascontiguousarray(np.broadcast_to(np.asarray(L0_attn_b_out, f32), (128, D)))
    ga_rep = np.ascontiguousarray(np.broadcast_to(np.asarray(L0_attn_gamma, f32), (128, D)))
    be_rep = np.ascontiguousarray(np.broadcast_to(np.asarray(L0_attn_beta, f32), (128, D)))

    inv_freq = (1.0 / (10000.0 ** (np.arange(0, 1024, 2, dtype=f32) / 1024.0))).astype(f32)
    freqs = np.arange(S, dtype=f32)[:, None] * inv_freq[None, :]
    cos_fm = np.ascontiguousarray(np.cos(freqs).T.astype(f32))
    sin_fm = np.ascontiguousarray(np.sin(freqs).T.astype(f32))

    Wo = np.ascontiguousarray(np.asarray(L0_attn_W_out, f32)[rho, :])
    common = dict(Wq=Wq, Wkv=Wkv, Wo=Wo,
                  Wg=Wg, Wu=Wu, Wd=Wd, bq_fm=bq_fm, bkv_fm=bkv_fm,
                  bo_rep=bo_rep, ga_rep=ga_rep, be_rep=be_rep,
                  cos_fm=cos_fm, sin_fm=sin_fm)
    X = np.asarray(L0_attn_X, f32)
    return [dict(common, X=np.ascontiguousarray(X[b])) for b in range(8)]


def kernel(**inputs):
    in_maps = _host_prep(**inputs)
    nc = build_program()
    res = run_bass_kernel_spmd(nc, in_maps, core_ids=list(range(8)))
    return np.stack([np.asarray(r["OUT"]) for r in res.results], axis=0)


# version 08
# speedup vs baseline: 1.0680x; 1.0680x over previous
import sys

sys.path.insert(0, "/opt/trn_rl_repo")
from contextlib import ExitStack

import numpy as np

import concourse.bass as bass
import concourse.mybir as mybir
from concourse.bass_utils import run_bass_kernel_spmd

F32 = mybir.dt.float32
BF16 = mybir.dt.bfloat16
U8 = mybir.dt.uint8
AF = mybir.ActivationFunctionType
ALU = mybir.AluOpType
AX = mybir.AxisListType

S = 4096
D = 2048
I = 8192
NG = 8
GT = 512
TPG = 4
NT = NG * TPG
DC = D // 128
EPS = 1e-5
INV_D = 1.0 / D
DEBUG_DUMP = False


class Sems:
    def __init__(self, nc, stack):
        self.nc, self.stack = nc, stack
        self.h, self.total, self.maxwait = {}, {}, {}

    def new(self, name):
        self.h[name] = self.stack.enter_context(self.nc.semaphore(name))
        self.total[name] = 0
        self.maxwait[name] = 0

    def inc(self, ins, name, n):
        self.total[name] += n
        return ins.then_inc(self.h[name], n)

    def wait(self, eng, name, val):
        if val <= 0:
            return
        self.maxwait[name] = max(self.maxwait[name], val)
        eng.wait_ge(self.h[name], val)

    def check(self):
        bad = {k: (self.maxwait[k], self.total[k])
               for k in self.total if self.maxwait[k] > self.total[k]}
        assert not bad, f"waits exceed emitted totals: {bad}"


def build_program():
    nc = bass.Bass("TRN2", target_bir_lowering=False, debug=False,
                   detect_race_conditions=False)

    X_d = nc.dram_tensor("X", [S, D], F32, kind="ExternalInput").ap()
    Wq_d = nc.dram_tensor("Wq", [D, D], F32, kind="ExternalInput").ap()
    Wkv_d = nc.dram_tensor("Wkv", [D, 2 * D], F32, kind="ExternalInput").ap()
    Wo_d = nc.dram_tensor("Wo", [D, D], F32, kind="ExternalInput").ap()
    Wg_d = nc.dram_tensor("Wg", [D, I], F32, kind="ExternalInput").ap()
    Wu_d = nc.dram_tensor("Wu", [D, I], F32, kind="ExternalInput").ap()
    Wd_d = nc.dram_tensor("Wd", [I, D], F32, kind="ExternalInput").ap()
    bqfm_d = nc.dram_tensor("bq_fm", [128, DC], F32, kind="ExternalInput").ap()
    bkvfm_d = nc.dram_tensor("bkv_fm", [128, 2 * DC], F32, kind="ExternalInput").ap()
    borep_d = nc.dram_tensor("bo_rep", [128, D], F32, kind="ExternalInput").ap()
    garep_d = nc.dram_tensor("ga_rep", [128, D], F32, kind="ExternalInput").ap()
    berep_d = nc.dram_tensor("be_rep", [128, D], F32, kind="ExternalInput").ap()
    cos_d = nc.dram_tensor("cos_fm", [512, S], F32, kind="ExternalInput").ap()
    sin_d = nc.dram_tensor("sin_fm", [512, S], F32, kind="ExternalInput").ap()
    OUT_d = nc.dram_tensor("OUT", [S, D], F32, kind="ExternalOutput").ap()

    wq_bf = nc.dram_tensor("wq_bf", [D, D], BF16, kind="Internal").ap()
    wkv_bf = nc.dram_tensor("wkv_bf", [D, 2 * D], BF16, kind="Internal").ap()
    wo_bf = nc.dram_tensor("wo_bf", [D, D], BF16, kind="Internal").ap()
    wg_bf = nc.dram_tensor("wg_bf", [D, I], BF16, kind="Internal").ap()
    wu_bf = nc.dram_tensor("wu_bf", [D, I], BF16, kind="Internal").ap()
    wd_bf = nc.dram_tensor("wd_bf", [I, D], BF16, kind="Internal").ap()
    xn_sp = nc.dram_tensor("xn_sp", [S, D], BF16, kind="Internal").ap()
    x2n_sp = nc.dram_tensor("x2n_sp", [S, D], BF16, kind="Internal").ap()
    x2_sp = nc.dram_tensor("x2_sp", [S, D], F32, kind="Internal").ap()
    o3_sp2 = nc.dram_tensor("o3_sp", [2 * D, GT], BF16, kind="Internal").ap()
    o3_sp = o3_sp2[0:D, :]
    if DEBUG_DUMP:
        dbg_xn = nc.dram_tensor("dbg_xn", [S, D], BF16, kind="ExternalOutput").ap()
        dbg_x2 = nc.dram_tensor("dbg_x2", [S, D], F32, kind="ExternalOutput").ap()
        dbg_x2n = nc.dram_tensor("dbg_x2n", [S, D], BF16, kind="ExternalOutput").ap()
        dbg_o3 = nc.dram_tensor("dbg_o3", [D, GT], BF16, kind="ExternalOutput").ap()
        dbg_ar = nc.dram_tensor("dbg_ar", [128, 212736], U8, kind="ExternalOutput").ap()

    with ExitStack() as stack:
        e = stack.enter_context
        sem = Sems(nc, stack)

        ARENA = 212736
        ar8 = e(nc.sbuf_tensor("arena", [128, ARENA], U8))
        arf = ar8.bitcast(F32)
        arb = ar8.bitcast(BF16)

        def vf(off, cols, parts=128):
            assert off % 4 == 0
            return arf[0:parts, off // 4: off // 4 + cols]

        def vb(off, cols, parts=128):
            assert off % 2 == 0
            return arb[0:parts, off // 2: off // 2 + cols]

        cstv = vf(0, 8)
        CONSTS = [0.0, 1.0, EPS, INV_D, -0.5]
        for ci, cvv in enumerate(CONSTS):
            nc.const_aps.register(F32, cvv, cstv[:, ci:ci + 1])
        A_fm = vf(32, DC)
        Abq = vf(96, DC)
        bq_fm = vf(160, DC)
        bkv_fm = vf(224, 2 * DC)
        stats = vf(352, 2 * NT)
        onesc = vf(608, 1)
        onesb = vb(24, 1)
        ones1 = vf(640, 128, parts=1)
        scrS = vf(1152, 1)
        mu2s = vf(1184, 1)
        r2s = vf(1216, 1)
        tl_s = vf(1248, 1)
        Agrp = vf(1280, 16 * NG)
        nk_sb = vf(1792, GT, parts=1)
        rn_sb = vf(3840, GT, parts=1)
        rnbc = vf(5888, GT)
        ga_rep = vb(7936, D)
        be_rep = vb(12032, D)
        sg_sb = vf(7936, 4 * GT)
        ah = sg_sb
        Xb = [vf(16128, D), vf(24320, D)]
        x2r = [vf(16128, D), vf(24320, D)]
        xc = vf(32512, D)
        o3tm = [vb(32512, D), vb(36608, D)]
        sqb = vf(40704, D)
        dcmp = [vb(40704 + 1024 * i, GT) for i in range(6)]
        srank = [vb(46848, GT), vb(47872, GT), vb(48896, GT), vb(49920, GT)]
        xn_tm = [vb(48896, D), vb(52992, D)]
        x2n_tm = xn_tm
        o3bf = vb(50944, 2048)
        Xn_fm = [vb(57088, DC * GT), vb(73472, DC * GT)]
        x2n_fm = vb(57088, DC * GT)
        wd_rg = [vb(73472, 4 * 1024), vb(81664, 4 * 1024)]
        wkv_rg = [vb(89856, DC * 256), vb(204544, DC * 256)]
        wq_rg = wkv_rg
        h_f = vf(89856, 4 * GT)
        cos_sb = vf(98048, 4 * GT)
        sin_sb = vf(106240, 4 * GT)
        x2b = [vf(98048, D), vf(106240, D)]
        wgu0 = vb(98048, DC * GT)
        t1 = vf(114432, 4 * GT)
        t2 = vf(122624, 4 * GT)
        sqn = [t1, t2]
        sqnb = [vb(114432, 4 * GT), vb(122624, 4 * GT)]
        Y_fm = vb(114432, DC * GT)
        wgu1 = vb(114432, DC * GT)
        K_fm = vf(130816, DC * GT)
        V_fm = vf(163584, DC * GT)
        wo_rg = [vb(130816, 4 * D), vb(147200, 4 * D)]
        bo_rep = vb(163584, D)
        h_s = vb(130816, (I // 128) * GT)
        wgu2 = vb(196352, DC * GT)
        pcst = vb(196352, 4096)
        wgu = [wgu0, wgu1, wgu2]

        for n in ["c_kv", "c_q", "c_o", "c_g", "c_u", "c_d",
                  "x1a", "x1b", "xf1", "ln", "sp1a", "sp1b", "fm1a", "fm1b",
                  "wkva", "wkvb", "kvmm", "kvev",
                  "csld", "cgb", "cssem", "sqv", "nmm", "rnv", "rnbc", "pa", "afin",
                  "vstat", "sstd", "vself",
                  "x2la", "x2lb", "wqa", "wqb", "woa", "wob2", "wob", "qmm", "yev",
                  "omm", "ev2", "v3", "sc3", "sp2a", "sp2b",
                  "fm3", "wgu0", "wgu1", "wgu2", "gmm", "umm", "silu", "hdn", "spar",
                  "dmm", "dmk", "o3ev", "o3sp", "o3t00", "o3t01", "o3t10", "o3t11",
                  "x2ra", "x2rb", "fin", "outwa", "outwb"]:
            sem.new(n)

        ps_t = e(nc.psum_tensor("ps", [128, 4096], F32))
        ps = ps_t.ap()

        def bank(b, w=1):
            return ps[:, b * 512:(b + w) * 512]

        block = e(nc.Block())

        def chunks_of(rows, cols):
            cb = min(cols, 4096)
            rb = max(128, (4096 // cb) * 128)
            return [(r, min(rb, rows - r), c, cb)
                    for r in range(0, rows, rb) for c in range(0, cols, cb)]

        kv_ch = chunks_of(D, 2 * D)
        q_ch = chunks_of(D, D)
        o_ch = chunks_of(D, D)
        g_ch = chunks_of(D, I)
        u_ch = chunks_of(D, I)
        d_ch = chunks_of(I, D)
        CKV = 32 * len(kv_ch)
        CQ = 32 * len(q_ch)
        CO = 32 * len(o_ch)
        CG = 32 * len(g_ch)
        CU = 32 * len(u_ch)
        CD = 32 * len(d_ch)

        @block.gpsimd
        def _(gp):
            ins = gp.dma_start(out=ga_rep, in_=garep_d)
            sem.inc(ins, "cgb", 16)
            sem.wait(gp, "cgb", sem.total["cgb"])
            ins = gp.dma_start(out=be_rep, in_=berep_d)
            sem.inc(ins, "cgb", 16)
            sem.wait(gp, "cgb", sem.total["cgb"])

            def do_cast(w32, wbf, chunks, sname):
                for (r, rp, c, cp) in chunks:
                    nb = rp // 128
                    src = w32[r:r + rp, c:c + cp].rearrange("(b p) c2 -> p b c2", p=128)
                    dst = pcst[:, 0:nb * cp].rearrange("p (b c2) -> p b c2", b=nb)
                    ins = gp.dma_start(out=dst, in_=src)
                    sem.inc(ins, sname, 16)
                    sem.wait(gp, sname, sem.total[sname])
                    dd = wbf[r:r + rp, c:c + cp].rearrange("(b p) c2 -> p b c2", p=128)
                    ins = gp.dma_start(out=dd, in_=dst)
                    sem.inc(ins, sname, 16)
                    sem.wait(gp, sname, sem.total[sname])

            do_cast(Wkv_d, wkv_bf, kv_ch, "c_kv")
            do_cast(Wq_d, wq_bf, q_ch, "c_q")
            do_cast(Wo_d, wo_bf, o_ch, "c_o")
            do_cast(Wg_d, wg_bf, g_ch, "c_g")
            do_cast(Wu_d, wu_bf, u_ch, "c_u")
            do_cast(Wd_d, wd_bf, d_ch, "c_d")
            sem.wait(gp, "pa", NG)
            ins = gp.dma_start(out=bo_rep, in_=borep_d)
            sem.inc(ins, "cgb", 16)
            sem.wait(gp, "cgb", sem.total["cgb"])

        @block.sync
        def _(sy):
            ins = sy.dma_start(out=bq_fm, in_=bqfm_d)
            sem.inc(ins, "csld", 16)
            ins = sy.dma_start(out=bkv_fm, in_=bkvfm_d)
            sem.inc(ins, "csld", 16)

            def xload1(t):
                sem.wait(sy, "xf1", t - 1)
                ins = sy.dma_start(out=Xb[t % 2], in_=X_d[t * 128:(t + 1) * 128, :])
                sem.inc(ins, "x1a" if t % 2 == 0 else "x1b", 16)

            def wkvload(g, j):
                J = 16 * g + j
                if J >= 2:
                    sem.wait(sy, "kvmm", 2 * J - 2)
                src = wkv_bf[:, j * 256:(j + 1) * 256].rearrange("(k p) c -> p k c", p=128)
                ins = sy.dma_start(out=wkv_rg[J % 2].rearrange("p (k c) -> p k c", k=DC), in_=src)
                sem.inc(ins, "wkva" if J % 2 == 0 else "wkvb", 16)

            def spill1(t):
                sem.wait(sy, "ln", t + 1)
                ins = sy.dma_start(out=xn_sp[t * 128:(t + 1) * 128, :], in_=xn_tm[t % 2])
                sem.inc(ins, "sp1a" if t % 2 == 0 else "sp1b", 16)

            xload1(0)
            xload1(1)
            for ti in range(TPG):
                if ti + 2 < TPG:
                    xload1(ti + 2)
                spill1(ti)
            sem.wait(sy, "c_kv", CKV)
            for g in range(NG):
                for ti in range(TPG):
                    if g + 1 < NG:
                        xload1(4 * (g + 1) + ti)
                        if ti >= 1:
                            spill1(4 * (g + 1) + ti - 1)
                        if ti == 3:
                            spill1(4 * (g + 1) + 3)
                    if ti == 0:
                        sem.wait(sy, "sp1a", 16 * (2 * g + 2))
                        sem.wait(sy, "sp1b", 16 * (2 * g + 2))
                        sem.wait(sy, "kvmm", 32 * (g - 1))
                        for c in range(DC // 2, DC):
                            ins = sy.dma_start_transpose(
                                Xn_fm[g % 2][:, c * GT:(c + 1) * GT],
                                xn_sp[g * GT:(g + 1) * GT, c * 128:(c + 1) * 128])
                            sem.inc(ins, "fm1a" if g % 2 == 0 else "fm1b", 16)
                    for j in range(4 * ti, 4 * ti + 4):
                        wkvload(g, j)

            for g in range(NG):
                if g == 0:
                    sem.wait(sy, "c_q", CQ)
                    sem.wait(sy, "c_o", CO)
                if g < 2:
                    sem.wait(sy, "kvmm", 32 * (NG - 1 + g))
                else:
                    sem.wait(sy, "qmm", 16 * (g - 1))
                sem.wait(sy, "sp1a", 16 * (NT // 2))
                sem.wait(sy, "sp1b", 16 * (NT // 2))
                for c in range(DC // 2, DC):
                    ins = sy.dma_start_transpose(
                        Xn_fm[g % 2][:, c * GT:(c + 1) * GT],
                        xn_sp[g * GT:(g + 1) * GT, c * 128:(c + 1) * 128])
                    sem.inc(ins, "fm1a" if g % 2 == 0 else "fm1b", 16)
                for j in range(8):
                    J = 8 * g + j
                    if J < 2:
                        sem.wait(sy, "kvmm", 32 * NG)
                    else:
                        sem.wait(sy, "qmm", 2 * J - 2)
                    src = wq_bf[:, j * 256:(j + 1) * 256].rearrange("(k p) c -> p k c", p=128)
                    ins = sy.dma_start(out=wq_rg[J % 2].rearrange("p (k c) -> p k c", k=DC), in_=src)
                    sem.inc(ins, "wqa" if J % 2 == 0 else "wqb", 16)
                for ti in range(2):
                    t = 4 * g + ti
                    if t >= 2:
                        sem.wait(sy, "ev2", 8 * ((t - 2) // 4) + [1, 2, 4, 6][(t - 2) % 4])
                    ins = sy.dma_start(out=Xb[t % 2], in_=X_d[t * 128:(t + 1) * 128, :])
                    sem.inc(ins, "x2la" if t % 2 == 0 else "x2lb", 16)
                for j in [0]:
                    J = 8 * g + j
                    if J < 2:
                        sem.wait(sy, "pa", NG)
                    else:
                        ref = J - 2
                        if ref % 4 == 3:
                            sem.wait(sy, "omm", 2 * (ref // 4) + 2)
                        else:
                            sem.wait(sy, "wob", 3 * (ref // 4) + ref % 4 + 1)
                    jj = j % 4
                    src = wo_bf[jj * 512:(jj + 1) * 512, :].rearrange("(k p) c -> p k c", p=128)
                    ins = sy.dma_start(out=wo_rg[J % 2].rearrange("p (k c) -> p k c", k=4), in_=src)
                    sem.inc(ins, "woa" if J % 2 == 0 else "wob2", 16)
                for j in [1]:
                    J = 8 * g + j
                    if J < 2:
                        sem.wait(sy, "pa", NG)
                    else:
                        ref = J - 2
                        if ref % 4 == 3:
                            sem.wait(sy, "omm", 2 * (ref // 4) + 2)
                        else:
                            sem.wait(sy, "wob", 3 * (ref // 4) + ref % 4 + 1)
                    jj = j % 4
                    src = wo_bf[jj * 512:(jj + 1) * 512, :].rearrange("(k p) c -> p k c", p=128)
                    ins = sy.dma_start(out=wo_rg[J % 2].rearrange("p (k c) -> p k c", k=4), in_=src)
                    sem.inc(ins, "woa" if J % 2 == 0 else "wob2", 16)
                for j in [2]:
                    J = 8 * g + j
                    if J < 2:
                        sem.wait(sy, "pa", NG)
                    else:
                        ref = J - 2
                        if ref % 4 == 3:
                            sem.wait(sy, "omm", 2 * (ref // 4) + 2)
                        else:
                            sem.wait(sy, "wob", 3 * (ref // 4) + ref % 4 + 1)
                    jj = j % 4
                    src = wo_bf[jj * 512:(jj + 1) * 512, :].rearrange("(k p) c -> p k c", p=128)
                    ins = sy.dma_start(out=wo_rg[J % 2].rearrange("p (k c) -> p k c", k=4), in_=src)
                    sem.inc(ins, "woa" if J % 2 == 0 else "wob2", 16)
                for j in [3]:
                    J = 8 * g + j
                    if J < 2:
                        sem.wait(sy, "pa", NG)
                    else:
                        ref = J - 2
                        if ref % 4 == 3:
                            sem.wait(sy, "omm", 2 * (ref // 4) + 2)
                        else:
                            sem.wait(sy, "wob", 3 * (ref // 4) + ref % 4 + 1)
                    jj = j % 4
                    src = wo_bf[jj * 512:(jj + 1) * 512, :].rearrange("(k p) c -> p k c", p=128)
                    ins = sy.dma_start(out=wo_rg[J % 2].rearrange("p (k c) -> p k c", k=4), in_=src)
                    sem.inc(ins, "woa" if J % 2 == 0 else "wob2", 16)
                for ti in range(2, TPG):
                    t = 4 * g + ti
                    sem.wait(sy, "ev2", 8 * ((t - 2) // 4) + [1, 2, 4, 6][(t - 2) % 4])
                    ins = sy.dma_start(out=Xb[t % 2], in_=X_d[t * 128:(t + 1) * 128, :])
                    sem.inc(ins, "x2la" if t % 2 == 0 else "x2lb", 16)
                for j in [4]:
                    J = 8 * g + j
                    if J < 2:
                        sem.wait(sy, "pa", NG)
                    else:
                        ref = J - 2
                        if ref % 4 == 3:
                            sem.wait(sy, "omm", 2 * (ref // 4) + 2)
                        else:
                            sem.wait(sy, "wob", 3 * (ref // 4) + ref % 4 + 1)
                    jj = j % 4
                    src = wo_bf[jj * 512:(jj + 1) * 512, :].rearrange("(k p) c -> p k c", p=128)
                    ins = sy.dma_start(out=wo_rg[J % 2].rearrange("p (k c) -> p k c", k=4), in_=src)
                    sem.inc(ins, "woa" if J % 2 == 0 else "wob2", 16)
                for j in [5]:
                    J = 8 * g + j
                    if J < 2:
                        sem.wait(sy, "pa", NG)
                    else:
                        ref = J - 2
                        if ref % 4 == 3:
                            sem.wait(sy, "omm", 2 * (ref // 4) + 2)
                        else:
                            sem.wait(sy, "wob", 3 * (ref // 4) + ref % 4 + 1)
                    jj = j % 4
                    src = wo_bf[jj * 512:(jj + 1) * 512, :].rearrange("(k p) c -> p k c", p=128)
                    ins = sy.dma_start(out=wo_rg[J % 2].rearrange("p (k c) -> p k c", k=4), in_=src)
                    sem.inc(ins, "woa" if J % 2 == 0 else "wob2", 16)
                for j in [6]:
                    J = 8 * g + j
                    if J < 2:
                        sem.wait(sy, "pa", NG)
                    else:
                        ref = J - 2
                        if ref % 4 == 3:
                            sem.wait(sy, "omm", 2 * (ref // 4) + 2)
                        else:
                            sem.wait(sy, "wob", 3 * (ref // 4) + ref % 4 + 1)
                    jj = j % 4
                    src = wo_bf[jj * 512:(jj + 1) * 512, :].rearrange("(k p) c -> p k c", p=128)
                    ins = sy.dma_start(out=wo_rg[J % 2].rearrange("p (k c) -> p k c", k=4), in_=src)
                    sem.inc(ins, "woa" if J % 2 == 0 else "wob2", 16)
                for j in [7]:
                    J = 8 * g + j
                    if J < 2:
                        sem.wait(sy, "pa", NG)
                    else:
                        ref = J - 2
                        if ref % 4 == 3:
                            sem.wait(sy, "omm", 2 * (ref // 4) + 2)
                        else:
                            sem.wait(sy, "wob", 3 * (ref // 4) + ref % 4 + 1)
                    jj = j % 4
                    src = wo_bf[jj * 512:(jj + 1) * 512, :].rearrange("(k p) c -> p k c", p=128)
                    ins = sy.dma_start(out=wo_rg[J % 2].rearrange("p (k c) -> p k c", k=4), in_=src)
                    sem.inc(ins, "woa" if J % 2 == 0 else "wob2", 16)
            sem.wait(sy, "c_g", CG)
            sem.wait(sy, "c_u", CU)

            def p3_fmload_hi(g):
                if g == 0:
                    last_even = NG - 2 if NG % 2 == 0 else NG - 1
                    sem.wait(sy, "qmm", 16 * (last_even + 1))
                else:
                    sem.wait(sy, "umm", 16 * g)
                sem.wait(sy, "sp2a", 32 * (NT // 2))
                sem.wait(sy, "sp2b", 32 * (NT // 2))
                for c in range(DC // 2, DC):
                    ins = sy.dma_start_transpose(
                        x2n_fm[:, c * GT:(c + 1) * GT],
                        x2n_sp[g * GT:(g + 1) * GT, c * 128:(c + 1) * 128])
                    sem.inc(ins, "fm3", 16)

            for g in range(NG):
                for c in range(16):
                    for s in range(2):
                        j = 32 * g + 2 * c + s
                        if j >= 3:
                            jc = j - 3
                            if jc % 2 == 0:
                                sem.wait(sy, "gmm", jc // 2 + 1)
                            else:
                                sem.wait(sy, "umm", jc // 2 + 1)
                        elif g == 0 and j == 0:
                            sem.wait(sy, "ev2", 2 * NT)
                            sem.wait(sy, "sp2a", 32 * (NT // 2))
                            sem.wait(sy, "sp2b", 32 * (NT // 2))
                        wsrc = (wg_bf if s == 0 else wu_bf)[:, c * 512:(c + 1) * 512]
                        src = wsrc.rearrange("(k p) c2 -> p k c2", p=128)
                        ins = sy.dma_start(out=wgu[j % 3].rearrange("p (k c2) -> p k c2", k=DC), in_=src)
                        sem.inc(ins, "wgu" + str(j % 3), 16)
                if g == 0:
                    sem.wait(sy, "c_d", CD)
                for h in range(2):
                    if h == 1 and g + 1 < NG:
                        p3_fmload_hi(g + 1)
                    for j in range(16):
                        J = 32 * g + 16 * h + j
                        if J >= 2:
                            ref = J - 2
                            if ref % 16 == 15:
                                sem.wait(sy, "dmm", ref // 16 + 1)
                            else:
                                sem.wait(sy, "dmk", 15 * (ref // 16) + ref % 16 + 1)
                        src = wd_bf[j * 512:(j + 1) * 512, h * 1024:(h + 1) * 1024]
                        src = src.rearrange("(k p) c -> p k c", p=128)
                        ins = sy.dma_start(out=wd_rg[J % 2].rearrange("p (k c) -> p k c", k=4), in_=src)
                        sem.inc(ins, "wkva" if J % 2 == 0 else "wkvb", 16)
                    for ee in range(2):
                        if g >= 2 and h == 0 and ee == 0:
                            for nm in ["o3t00", "o3t01", "o3t10", "o3t11"]:
                                sem.wait(sy, nm, 32 * (g - 1))
                        sem.wait(sy, "o3ev", 4 * g + 2 * h + ee + 1)
                        base = (g % 2) * D
                        ins = sy.dma_start(
                            out=o3_sp2[base + h * 1024 + ee * 512:base + h * 1024 + (ee + 1) * 512, :]
                            .rearrange("(c p) t -> p c t", p=128),
                            in_=o3bf.rearrange("p (c t) -> p c t", c=4))
                        sem.inc(ins, "o3sp", 16)
                def x2r_load(t):
                    if t < 2:
                        sem.wait(sy, "ev2", 2 * NT)
                        sem.wait(sy, "sp2a", 32 * (NT // 2))
                        sem.wait(sy, "sp2b", 32 * (NT // 2))
                    else:
                        sem.wait(sy, "fin", t - 1)
                        sem.wait(sy, "outwa" if t % 2 == 0 else "outwb",
                                 16 * (t // 2))
                    ins = sy.dma_start(out=x2r[t % 2], in_=x2_sp[t * 128:(t + 1) * 128, :])
                    sem.inc(ins, "x2ra" if t % 2 == 0 else "x2rb", 16)

                def out_write(t):
                    sem.wait(sy, "fin", t + 1)
                    ins = sy.dma_start(out=OUT_d[t * 128:(t + 1) * 128, :], in_=x2r[t % 2])
                    sem.inc(ins, "outwa" if t % 2 == 0 else "outwb", 16)

                x2r_load(4 * g)
                x2r_load(4 * g + 1)
                out_write(4 * g)
                x2r_load(4 * g + 2)
                out_write(4 * g + 1)
                x2r_load(4 * g + 3)
                out_write(4 * g + 2)
                out_write(4 * g + 3)
            if DEBUG_DUMP:
                ins = sy.dma_start(out=dbg_xn, in_=xn_sp)
                sem.inc(ins, "csld", 16)
                ins = sy.dma_start(out=dbg_x2, in_=x2_sp)
                sem.inc(ins, "csld", 16)
                ins = sy.dma_start(out=dbg_x2n, in_=x2n_sp)
                sem.inc(ins, "csld", 16)
                ins = sy.dma_start(out=dbg_o3, in_=o3_sp)
                sem.inc(ins, "csld", 16)
                ins = sy.dma_start(out=dbg_ar, in_=ar8[:])
                sem.inc(ins, "csld", 16)
            for n2 in ["x1a", "x1b", "x2la", "x2lb", "sp1a", "sp1b", "sp2a", "sp2b",
                       "wkva", "wkvb", "wqa", "wqb", "woa", "wob2",
                       "wgu0", "wgu1", "wgu2", "x2ra", "x2rb", "o3sp", "csld",
                       "outwa", "outwb"]:
                sem.wait(sy, n2, sem.total[n2])

        @block.scalar
        def _(sc):
            sem.wait(sc, "csld", 32)
            def sqrt_pair(t):
                sem.wait(sc, "vstat", 4 * t + 1)
                ins = sc.activation(stats[:, t:t + 1], scrS, AF.Identity, scale=INV_D)
                sem.inc(ins, "sstd", 1)
                sem.wait(sc, "vstat", 4 * t + 2)
                sc.activation(tl_s, scrS, AF.Ln, bias=EPS, scale=INV_D)
                ins = sc.activation(stats[:, NT + t:NT + t + 1], tl_s, AF.Exp, scale=-0.5)
                sem.inc(ins, "sstd", 1)
                sem.wait(sc, "vstat", 4 * t + 3)
                ins = sc.activation(mu2s, scrS, AF.Identity, scale=INV_D)
                sem.inc(ins, "sstd", 1)
                sem.wait(sc, "vstat", 4 * t + 4)
                sc.activation(tl_s, scrS, AF.Ln, bias=EPS, scale=INV_D)
                ins = sc.activation(r2s, tl_s, AF.Exp, scale=-0.5)
                sem.inc(ins, "sstd", 1)

            def kv_evac(gg, m):
                sem.wait(sc, "kvmm", 32 * gg + m + 1)
                if m == 0:
                    sem.wait(sc, "pa", gg)
                dst = K_fm[:, m * GT:(m + 1) * GT] if m < DC else \
                    V_fm[:, (m - DC) * GT:(m - DC + 1) * GT]
                ins = sc.activation(dst, bank(m % 2), AF.Identity,
                                    bias=bkv_fm[:, m:m + 1], scale=1.0)
                sem.inc(ins, "kvev", 1)

            for g in range(NG + 1):
                for ti in range(TPG):
                    if g < NG:
                        sqrt_pair(4 * g + ti)
                    if g < NG and ti == 3:
                        sem.wait(sc, "sp1a", 16 * (2 * g + 2))
                        sem.wait(sc, "sp1b", 16 * (2 * g + 2))
                        sem.wait(sc, "kvmm", 32 * (g - 1))
                        for c in range(DC // 2):
                            ins = sc.dma_start_transpose(
                                Xn_fm[g % 2][:, c * GT:(c + 1) * GT],
                                xn_sp[g * GT:(g + 1) * GT, c * 128:(c + 1) * 128])
                            sem.inc(ins, "fm1a" if g % 2 == 0 else "fm1b", 16)
                    if g >= 1:
                        for m in range(8 * ti, 8 * ti + 8):
                            kv_evac(g - 1, m)
                if g >= 1:
                    gg = g - 1
                    sem.wait(sc, "nmm", 8 * gg + 8)
                    ins = sc.copy(nk_sb, ps[0:1, 2 * 512:2 * 512 + GT])
                    sem.inc(ins, "rnv", 1)
                    sem.wait(sc, "rnv", 3 * gg + 2)
                    sc.activation(rn_sb, rn_sb, AF.Ln)
                    ins = sc.activation(rn_sb, rn_sb, AF.Exp, scale=-0.5)
                    sem.inc(ins, "rnv", 1)
                if g < NG:
                    sem.wait(sc, "sqv", 8 * (g - 1) + 1)
                    ins = sc.dma_start(out=cos_sb.rearrange("p (c t) -> p c t", c=4),
                                       in_=cos_d[:, g * GT:(g + 1) * GT]
                                       .rearrange("(c p) t -> p c t", p=128))
                    sem.inc(ins, "cssem", 16)
                    ins = sc.dma_start(out=sin_sb.rearrange("p (c t) -> p c t", c=4),
                                       in_=sin_d[:, g * GT:(g + 1) * GT]
                                       .rearrange("(c p) t -> p c t", p=128))
                    sem.inc(ins, "cssem", 16)

            def p2_fmload(g):
                sem.wait(sc, "qmm", 16 * (g - 1))
                for c in range(DC // 2):
                    ins = sc.dma_start_transpose(
                        Xn_fm[g % 2][:, c * GT:(c + 1) * GT],
                        xn_sp[g * GT:(g + 1) * GT, c * 128:(c + 1) * 128])
                    sem.inc(ins, "fm1a" if g % 2 == 0 else "fm1b", 16)

            sem.wait(sc, "sp1a", 16 * (NT // 2))
            sem.wait(sc, "sp1b", 16 * (NT // 2))
            p2_fmload(0)
            for t in range(NT):
                sc.activation(stats[:, t:t + 1], stats[:, t:t + 1], AF.Identity,
                              scale=stats[:, NT + t:NT + t + 1])
                ins = sc.activation(stats[:, NT + t:NT + t + 1],
                                    stats[:, NT + t:NT + t + 1], AF.Identity, bias=1.0)
                if t == NT - 1:
                    sem.inc(ins, "sc3", 1)
            for g in range(NG):
                if g == 0:
                    sem.wait(sc, "afin", 1)
                for m in range(DC):
                    sem.wait(sc, "qmm", 16 * g + m + 1)
                    ins = sc.activation(Y_fm[:, m * GT:(m + 1) * GT], bank(m % 2), AF.Identity,
                                        bias=Abq[:, m:m + 1], scale=A_fm[:, m:m + 1])
                    sem.inc(ins, "yev", 1)
                if g + 1 < NG:
                    p2_fmload(g + 1)
                for ti in range(TPG):
                    t = 4 * g + ti
                    sem.wait(sc, "v3", 2 * t + 1)
                    ins = sc.activation(mu2s, scrS, AF.Identity, scale=INV_D)
                    sem.inc(ins, "sc3", 1)
                    sem.wait(sc, "v3", 2 * t + 2)
                    sc.activation(tl_s, scrS, AF.Ln, bias=EPS, scale=INV_D)
                    ins = sc.activation(r2s, tl_s, AF.Exp, scale=-0.5)
                    sem.inc(ins, "sc3", 1)
                    sem.wait(sc, "ev2", 8 * (t // 4) + [3, 5, 7, 8][t % 4])
                    s2n = "sp2a" if t % 2 == 0 else "sp2b"
                    ins = sc.dma_start(out=x2_sp[t * 128:(t + 1) * 128, :], in_=x2b[t % 2])
                    sem.inc(ins, s2n, 16)
                    ins = sc.dma_start(out=x2n_sp[t * 128:(t + 1) * 128, :], in_=x2n_tm[t % 2])
                    sem.inc(ins, s2n, 16)

            def p3_fmload(g):
                sem.wait(sc, "umm", 16 * g)
                for c in range(DC if g == 0 else DC // 2):
                    ins = sc.dma_start_transpose(
                        x2n_fm[:, c * GT:(c + 1) * GT],
                        x2n_sp[g * GT:(g + 1) * GT, c * 128:(c + 1) * 128])
                    sem.inc(ins, "fm3", 16)

            sem.wait(sc, "sp2a", 32 * (NT // 2))
            sem.wait(sc, "sp2b", 32 * (NT // 2))
            p3_fmload(0)
            for g in range(NG):
                for c in range(16):
                    C = 16 * g + c
                    sem.wait(sc, "gmm", C + 1)
                    sem.wait(sc, "spar", C)
                    ins = sc.activation(sg_sb, bank(0, 4), AF.Silu)
                    sem.inc(ins, "silu", 1)
                    sem.wait(sc, "hdn", C + 1)
                    ins = sc.activation(ah, h_f, AF.Abs)
                    sem.inc(ins, "silu", 1)
                if g + 1 < NG:
                    p3_fmload(g + 1)
                for h in range(2):
                    sem.wait(sc, "dmm", 2 * g + h + 1)
                    for ee in range(2):
                        sem.wait(sc, "o3sp", 16 * (4 * g + 2 * h + ee))
                        ins = sc.copy(o3bf, ps[:, ee * 2048:(ee + 1) * 2048])
                        sem.inc(ins, "o3ev", 1)
                sem.wait(sc, "o3sp", 16 * (4 * g + 4))
                for ti in range(TPG):
                    t = 4 * g + ti
                    if t < 2 and g == 0:
                        sem.wait(sc, "ev2", 2 * NT)
                    elif t >= 2:
                        sem.wait(sc, "fin", t - 1)
                    base = (g % 2) * D
                    for h in range(2):
                        ins = sc.dma_start_transpose(
                            o3tm[t % 2][:, h * 1024:(h + 1) * 1024],
                            o3_sp2[base + h * 1024:base + (h + 1) * 1024,
                                   ti * 128:(ti + 1) * 128])
                        sem.inc(ins, "o3t" + str(t % 2) + str(h), 16)
            for n2 in ["fm1a", "fm1b", "fm3", "cssem", "o3t00", "o3t01", "o3t10", "o3t11"]:
                sem.wait(sc, n2, sem.total[n2])

        @block.tensor
        def _(te):
            for g in range(NG):
                sem.wait(te, "fm1a" if g % 2 == 0 else "fm1b", 16 * 16 * (g // 2 + 1))
                for m in range(32):
                    J = 16 * g + m // 2
                    sem.wait(te, "wkva" if J % 2 == 0 else "wkvb", 16 * (J // 2 + 1))
                    sem.wait(te, "kvev", 32 * g + m - 1)
                    msel = m % 2
                    ins = None
                    for k in range(DC):
                        lhsT = wkv_rg[J % 2][:, k * 256 + msel * 128:k * 256 + msel * 128 + 128]
                        rhs = Xn_fm[g % 2][:, k * GT:(k + 1) * GT]
                        ins = te.matmul(bank(m % 2), lhsT, rhs,
                                        start=(k == 0), stop=(k == DC - 1))
                    sem.inc(ins, "kvmm", 1)
                for tn in range(2):
                    pb = 2 + tn
                    if tn == 0:
                        sem.wait(te, "rnv", 3 * (g - 1) + 1)
                    else:
                        sem.wait(te, "rnv", 3 * (g - 1) + 2)
                    for q in range(4):
                        sem.wait(te, "sqv", 8 * g + 4 * tn + q + 1)
                        ins = None
                        for c in range(4):
                            ins = te.matmul(ps[0:1, pb * 512:pb * 512 + GT],
                                            onesb, sqnb[q % 2][:, c * GT:(c + 1) * GT],
                                            start=(q == 0 and c == 0),
                                            stop=(q == 3 and c == 3))
                        sem.inc(ins, "nmm", 1)
                sem.wait(te, "rnv", 3 * g + 3)
                sem.wait(te, "pa", g)
                ins = te.matmul(bank(4), ones1, rn_sb, start=True, stop=True)
                sem.inc(ins, "rnbc", 1)

            sem.wait(te, "pa", NG)
            for g in range(NG):
                sem.wait(te, "fm1a" if g % 2 == 0 else "fm1b",
                         16 * 16 * (NG // 2 + g // 2 + 1))
                for m in range(DC):
                    J = 8 * g + m // 2
                    if m == 0 and g >= 1:
                        sem.wait(te, "ev2", 8 * (g - 1) + 4)
                    sem.wait(te, "wqa" if J % 2 == 0 else "wqb", 16 * (J // 2 + 1))
                    sem.wait(te, "yev", 16 * g + m - 1)
                    msel = m % 2
                    ins = None
                    for k in range(DC):
                        lhsT = wq_rg[J % 2][:, k * 256 + msel * 128:k * 256 + msel * 128 + 128]
                        rhs = Xn_fm[g % 2][:, k * GT:(k + 1) * GT]
                        ins = te.matmul(bank(m % 2), lhsT, rhs,
                                        start=(k == 0), stop=(k == DC - 1))
                    sem.inc(ins, "qmm", 1)
                sem.wait(te, "yev", 16 * (g + 1))
                for mtp in range(2):
                    last = {}
                    for j in range(4):
                        J = 8 * g + 4 * mtp + j
                        sem.wait(te, "woa" if J % 2 == 0 else "wob2", 16 * (J // 2 + 1))
                        for kk in range(4):
                            k = 4 * j + kk
                            for q in range(2):
                                mt = 2 * mtp + q
                                t = 4 * g + mt
                                if j == 0 and kk == 0 and t >= 2:
                                    sem.wait(te, "ev2", 8 * ((t - 2) // 4) + [1, 2, 4, 6][(t - 2) % 4])
                                pset = (mt % 2) * 4
                                lhsT = Y_fm[:, k * GT + mt * 128:k * GT + mt * 128 + 128]
                                for n in range(4):
                                    last[mt] = te.matmul(
                                        ps[:, (pset + n) * 512:(pset + n + 1) * 512],
                                        lhsT, wo_rg[J % 2][:, kk * D + n * 512:kk * D + (n + 1) * 512],
                                        start=(k == 0), stop=(k == DC - 1),
                                        skip_group_check=True)
                        if j < 3:
                            sem.inc(last[2 * mtp + 1], "wob", 1)
                    for q in range(2):
                        sem.inc(last[2 * mtp + q], "omm", 1)

            sem.wait(te, "ev2", 2 * NT)
            for g in range(NG):
                sem.wait(te, "fm3", 16 * (16 * g + 16))
                sem.wait(te, "o3ev", 4 * g)
                for c in range(16):
                    C = 16 * g + c
                    j0 = 2 * C
                    sem.wait(te, "wgu" + str(j0 % 3), 16 * (j0 // 3 + 1))
                    sem.wait(te, "silu", 2 * C - 1)
                    ins = None
                    for m in range(4):
                        for k in range(DC):
                            lhsT = wgu[(2 * C) % 3][:, k * GT + m * 128:k * GT + m * 128 + 128]
                            ins = te.matmul(bank(m), lhsT, x2n_fm[:, k * GT:(k + 1) * GT],
                                            start=(k == 0), stop=(k == DC - 1))
                    sem.inc(ins, "gmm", 1)
                    j1 = 2 * C + 1
                    sem.wait(te, "wgu" + str(j1 % 3), 16 * (j1 // 3 + 1))
                    sem.wait(te, "hdn", C)
                    ins = None
                    for m in range(4):
                        for k in range(DC):
                            lhsT = wgu[(2 * C + 1) % 3][:, k * GT + m * 128:k * GT + m * 128 + 128]
                            ins = te.matmul(bank(4 + m), lhsT, x2n_fm[:, k * GT:(k + 1) * GT],
                                            start=(k == 0), stop=(k == DC - 1))
                    sem.inc(ins, "umm", 1)
                sem.wait(te, "spar", 16 * g + 16)
                for h in range(2):
                    if not (g == 0 and h == 0):
                        sem.wait(te, "o3ev", 4 * g + 2 * h)
                    ins = None
                    for j in range(16):
                        J = 32 * g + 16 * h + j
                        sem.wait(te, "wkva" if J % 2 == 0 else "wkvb",
                                 16 * (8 * NG + J // 2 + 1))
                        for kk in range(4):
                            k = 4 * j + kk
                            rhs = h_s[:, k * GT:(k + 1) * GT]
                            for m in range(8):
                                lhsT = wd_rg[J % 2][:, kk * 1024 + m * 128:kk * 1024 + (m + 1) * 128]
                                ins = te.matmul(bank(m), lhsT, rhs,
                                                start=(k == 0), stop=(k == 63),
                                                skip_group_check=True)
                        if j < 15:
                            sem.inc(ins, "dmk", 1)
                    sem.inc(ins, "dmm", 1)

        @block.vector
        def _(ve):
            ve.memset(onesc, 1.0)
            ve.memset(onesb, 1.0)
            ve.memset(ones1, 1.0)
            for ci, cvv in enumerate(CONSTS):
                ve.memset(cstv[:, ci:ci + 1], cvv)
            sem.wait(ve, "csld", 32)
            sem.wait(ve, "cgb", 32)
            for g in range(NG + 1):
                if g < NG:
                    for ti in range(TPG):
                        t = 4 * g + ti
                        sem.wait(ve, "x1a" if t % 2 == 0 else "x1b", 16 * (t // 2 + 1))
                        X = Xb[t % 2]
                        ins = ve.reduce_sum(scrS, X, axis=AX.X)
                        sem.inc(ins, "vstat", 1)
                        sem.wait(ve, "sstd", 4 * t + 1)
                        ins = ve.tensor_scalar(xc, X, stats[:, t:t + 1], None, ALU.subtract)
                        sem.inc(ins, "xf1", 1)
                        ins = ve.scalar_tensor_tensor(sqb, xc, 1.0, xc, ALU.mult, ALU.mult,
                                                      accum_out=scrS)
                        sem.inc(ins, "vstat", 1)
                        sem.wait(ve, "sstd", 4 * t + 2)
                        ins = ve.tensor_scalar(xc, xc, stats[:, NT + t:NT + t + 1], None, ALU.mult)
                        ins = ve.reduce_sum(scrS, xc, axis=AX.X)
                        sem.inc(ins, "vstat", 1)
                        sem.wait(ve, "sstd", 4 * t + 3)
                        ve.tensor_scalar(xc, xc, mu2s, None, ALU.subtract)
                        ins = ve.scalar_tensor_tensor(sqb, xc, 1.0, xc, ALU.mult, ALU.mult,
                                                      accum_out=scrS)
                        sem.inc(ins, "vstat", 1)
                        sem.wait(ve, "sstd", 4 * t + 4)
                        ve.scalar_tensor_tensor(xc, xc, r2s, ga_rep, ALU.mult, ALU.mult)
                        if t >= 2:
                            sem.wait(ve, "sp1a" if t % 2 == 0 else "sp1b", 16 * (t // 2))
                        ins = ve.tensor_tensor(xn_tm[t % 2], xc, be_rep, ALU.add)
                        sem.inc(ins, "ln", 1)
                if g >= 1:
                    gg = g - 1
                    sem.wait(ve, "nmm", 8 * gg)
                    sem.wait(ve, "cssem", 32 * (gg + 1))
                    for tn in range(2):
                        T_ = K_fm if tn == 0 else V_fm
                        sem.wait(ve, "kvev", 32 * gg + 16 * tn + 8)
                        ev = T_[:, 0:4 * GT]
                        od = T_[:, 4 * GT:8 * GT]
                        ve.tensor_tensor(t1, ev, cos_sb, ALU.mult)
                        ve.tensor_tensor(t2, od, sin_sb, ALU.mult)
                        ve.tensor_tensor(t1, t1, t2, ALU.subtract)
                        ve.tensor_tensor(t2, od, cos_sb, ALU.mult)
                        ve.tensor_tensor(od, ev, sin_sb, ALU.mult)
                        ve.tensor_tensor(od, od, t2, ALU.add)
                        ve.tensor_copy(ev, t1)
                    for tn in range(2):
                        T_ = K_fm if tn == 0 else V_fm
                        sem.wait(ve, "kvev", 32 * gg + 16 + 16 * tn)
                        for q in range(4):
                            qb = 8 * gg + 4 * tn + q
                            sem.wait(ve, "nmm", qb - 1)
                            sl = T_[:, q * 4 * GT:(q + 1) * 4 * GT]
                            ins = ve.tensor_tensor(sqnb[q % 2], sl, sl, ALU.mult)
                            sem.inc(ins, "sqv", 1)
                    sem.wait(ve, "rnv", 3 * gg + 1)
                    ins = ve.tensor_tensor(rn_sb, nk_sb, ps[0:1, 3 * 512:3 * 512 + GT], ALU.mult)
                    sem.inc(ins, "rnv", 1)
                    sem.wait(ve, "rnbc", gg + 1)
                    ve.tensor_copy(rnbc, bank(4))
                    ve.tensor_tensor(K_fm, K_fm, V_fm, ALU.mult)
                    rb = rnbc.rearrange("p (c t) -> p c t", c=1).broadcast_to([128, DC, GT])
                    ve.tensor_tensor(K_fm.rearrange("p (c t) -> p c t", c=DC),
                                     K_fm.rearrange("p (c t) -> p c t", c=DC), rb, ALU.mult)
                    ins = ve.reduce_sum(Agrp[:, 16 * gg:16 * gg + 16],
                                        K_fm.rearrange("p (c t) -> p c t", c=DC), axis=AX.X)
                    sem.inc(ins, "pa", 1)
            ins = ve.tensor_copy(xc[:, 0:16], Agrp[:, 0:16])
            sem.inc(ins, "vself", 1)
            vs = 1
            sem.wait(ve, "vself", vs)
            idxs = list(range(NG))
            while len(idxs) > 1:
                nxt = []
                ins = None
                for i in range(0, len(idxs) - 1, 2):
                    a_, b_ = idxs[i], idxs[i + 1]
                    ins = ve.tensor_tensor(Agrp[:, 16 * a_:16 * a_ + 16],
                                           Agrp[:, 16 * a_:16 * a_ + 16],
                                           Agrp[:, 16 * b_:16 * b_ + 16], ALU.add)
                    nxt.append(a_)
                if len(idxs) % 2 == 1:
                    nxt.append(idxs[-1])
                sem.inc(ins, "vself", 1)
                vs += 1
                sem.wait(ve, "vself", vs)
                idxs = nxt
            ins = ve.tensor_scalar(A_fm, Agrp[:, 16 * idxs[0]:16 * idxs[0] + 16],
                                   0.5, 0.5, ALU.mult, ALU.add)
            sem.inc(ins, "vself", 1)
            vs += 1
            sem.wait(ve, "vself", vs)
            ins = ve.tensor_tensor(Abq, A_fm, bq_fm, ALU.mult)
            sem.inc(ins, "afin", 1)

            sem.wait(ve, "cgb", 48)
            sem.wait(ve, "sc3", 1)
            EV_E = [1, 2, 4, 6]
            EV_L = [3, 5, 7, 8]

            def p2_evac(g, mt):
                t = 4 * g + mt
                sem.wait(ve, "omm", t + 1)
                sem.wait(ve, "x2la" if t % 2 == 0 else "x2lb", 16 * (t // 2 + 1))
                X = Xb[t % 2]
                x2 = x2b[t % 2]
                pset = (mt % 2) * 4
                o2 = ps[:, pset * 512:(pset + 4) * 512]
                if t >= 2:
                    sem.wait(ve, "sp2a" if t % 2 == 0 else "sp2b", 32 * (t // 2))
                ve.tensor_tensor(xc, o2, bo_rep, ALU.add)
                ve.tensor_scalar(x2, X, stats[:, NT + t:NT + t + 1],
                                 stats[:, t:t + 1], ALU.mult, ALU.subtract)
                ins = ve.tensor_tensor(x2, x2, xc, ALU.add)
                sem.inc(ins, "ev2", 1)

            def p2_ln3(g, mt):
                t = 4 * g + mt
                x2 = x2b[t % 2]
                ins = ve.reduce_sum(scrS, x2, axis=AX.X)
                sem.inc(ins, "v3", 1)
                sem.wait(ve, "sc3", 1 + 2 * t + 1)
                ve.tensor_scalar(xc, x2, mu2s, None, ALU.subtract)
                ins = ve.scalar_tensor_tensor(sqb, xc, 1.0, xc, ALU.mult, ALU.mult,
                                              accum_out=scrS)
                sem.inc(ins, "v3", 1)
                sem.wait(ve, "sc3", 1 + 2 * t + 2)
                if t < 2:
                    sem.wait(ve, "sp1a", 16 * (NT // 2))
                    sem.wait(ve, "sp1b", 16 * (NT // 2))
                ins = ve.tensor_scalar(x2n_tm[t % 2], xc, r2s, None, ALU.mult)
                sem.inc(ins, "ev2", 1)

            for g in range(NG):
                p2_evac(g, 0)
                p2_evac(g, 1)
                p2_ln3(g, 0)
                p2_evac(g, 2)
                p2_ln3(g, 1)
                p2_evac(g, 3)
                p2_ln3(g, 2)
                p2_ln3(g, 3)

            for g in range(NG):
                for c in range(16):
                    C = 16 * g + c
                    sem.wait(ve, "silu", 2 * C + 1)
                    sem.wait(ve, "umm", C + 1)
                    ins = ve.tensor_tensor(h_f, sg_sb, bank(4, 4), ALU.mult)
                    sem.inc(ins, "hdn", 1)
                    sem.wait(ve, "silu", 2 * C + 2)
                    qv = [ah[:, i * GT:(i + 1) * GT] for i in range(4)]
                    ve.tensor_tensor(dcmp[0], qv[0], qv[1], ALU.is_ge)
                    ve.tensor_tensor(dcmp[1], qv[0], qv[2], ALU.is_ge)
                    ve.tensor_tensor(dcmp[2], qv[0], qv[3], ALU.is_ge)
                    ve.tensor_tensor(dcmp[3], qv[1], qv[2], ALU.is_ge)
                    ve.tensor_tensor(dcmp[4], qv[1], qv[3], ALU.is_ge)
                    ve.tensor_tensor(dcmp[5], qv[2], qv[3], ALU.is_ge)
                    ve.tensor_tensor(srank[0], dcmp[0], dcmp[1], ALU.add)
                    ve.tensor_tensor(srank[0], srank[0], dcmp[2], ALU.add)
                    ve.tensor_tensor(srank[1], dcmp[3], dcmp[4], ALU.add)
                    ve.tensor_tensor(srank[1], srank[1], dcmp[0], ALU.subtract)
                    ve.tensor_tensor(srank[2], dcmp[1], dcmp[3], ALU.add)
                    ve.tensor_tensor(srank[2], srank[2], dcmp[5], ALU.subtract)
                    ve.tensor_tensor(srank[3], dcmp[2], dcmp[4], ALU.add)
                    ve.tensor_tensor(srank[3], srank[3], dcmp[5], ALU.add)
                    ve.tensor_scalar(srank[0], srank[0], 2.0, None, ALU.is_ge)
                    ve.tensor_scalar(srank[1], srank[1], 1.0, None, ALU.is_ge)
                    ve.tensor_scalar(srank[2], srank[2], 0.0, None, ALU.is_le)
                    ve.tensor_scalar(srank[3], srank[3], 1.0, None, ALU.is_le)
                    if c == 0:
                        sem.wait(ve, "dmm", 2 * g)
                    ins = None
                    for i2 in range(4):
                        ins = ve.tensor_tensor(h_s[:, (4 * c + i2) * GT:(4 * c + i2 + 1) * GT],
                                               h_f[:, i2 * GT:(i2 + 1) * GT], srank[i2], ALU.mult)
                    sem.inc(ins, "spar", 1)
                for ti in range(TPG):
                    t = 4 * g + ti
                    sem.wait(ve, "o3t" + str(t % 2) + "0", 16 * (2 * g + ti // 2 + 1))
                    sem.wait(ve, "o3t" + str(t % 2) + "1", 16 * (2 * g + ti // 2 + 1))
                    sem.wait(ve, "x2ra" if t % 2 == 0 else "x2rb", 16 * (t // 2 + 1))
                    if t >= 2:
                        sem.wait(ve, "outwa" if t % 2 == 0 else "outwb", 16 * (t // 2))
                    ins = ve.tensor_tensor(x2r[t % 2], x2r[t % 2], o3tm[t % 2], ALU.add)
                    sem.inc(ins, "fin", 1)

        sem.check()
    return nc


def _host_prep(L0_attn_X, L0_attn_W_in, L0_attn_b_in, L0_attn_W_out, L0_attn_b_out,
               L0_attn_gamma, L0_attn_beta, L0_ffn_w_gate, L0_ffn_w_up, L0_ffn_w_down):
    f32 = np.float32
    W_in = np.asarray(L0_attn_W_in, f32)
    b_in = np.asarray(L0_attn_b_in, f32)

    rho = np.concatenate([np.arange(0, 1024, 2), np.arange(1, 1024, 2),
                          np.arange(1024, 2048)])
    Wq = np.ascontiguousarray(W_in[:, 0:D][:, rho])
    Wk = W_in[:, D:2 * D][:, rho]
    Wv = W_in[:, 2 * D:3 * D][:, rho]
    Wkv = np.ascontiguousarray(np.concatenate([Wk, Wv], axis=1))
    bq = b_in[0:D][rho]
    bkv = np.concatenate([b_in[D:2 * D][rho], b_in[2 * D:3 * D][rho]])

    b_idx = np.arange(16)[:, None, None]
    k_idx = np.arange(4)[None, :, None]
    r_idx = np.arange(128)[None, None, :]
    P = (4 * (128 * b_idx + r_idx) + k_idx).reshape(-1)
    Wg = np.ascontiguousarray(np.asarray(L0_ffn_w_gate, f32)[:, P])
    Wu = np.ascontiguousarray(np.asarray(L0_ffn_w_up, f32)[:, P])
    Wd = np.ascontiguousarray(np.asarray(L0_ffn_w_down, f32)[P, :])

    bq_fm = np.ascontiguousarray(bq.reshape(DC, 128).T)
    bkv_fm = np.ascontiguousarray(bkv.reshape(2 * DC, 128).T)
    bo_rep = np.ascontiguousarray(np.broadcast_to(np.asarray(L0_attn_b_out, f32), (128, D)))
    ga_rep = np.ascontiguousarray(np.broadcast_to(np.asarray(L0_attn_gamma, f32), (128, D)))
    be_rep = np.ascontiguousarray(np.broadcast_to(np.asarray(L0_attn_beta, f32), (128, D)))

    inv_freq = (1.0 / (10000.0 ** (np.arange(0, 1024, 2, dtype=f32) / 1024.0))).astype(f32)
    freqs = np.arange(S, dtype=f32)[:, None] * inv_freq[None, :]
    cos_fm = np.ascontiguousarray(np.cos(freqs).T.astype(f32))
    sin_fm = np.ascontiguousarray(np.sin(freqs).T.astype(f32))

    Wo = np.ascontiguousarray(np.asarray(L0_attn_W_out, f32)[rho, :])
    common = dict(Wq=Wq, Wkv=Wkv, Wo=Wo,
                  Wg=Wg, Wu=Wu, Wd=Wd, bq_fm=bq_fm, bkv_fm=bkv_fm,
                  bo_rep=bo_rep, ga_rep=ga_rep, be_rep=be_rep,
                  cos_fm=cos_fm, sin_fm=sin_fm)
    X = np.asarray(L0_attn_X, f32)
    return [dict(common, X=np.ascontiguousarray(X[b])) for b in range(8)]


def kernel(**inputs):
    in_maps = _host_prep(**inputs)
    nc = build_program()
    res = run_bass_kernel_spmd(nc, in_maps, core_ids=list(range(8)))
    return np.stack([np.asarray(r["OUT"]) for r in res.results], axis=0)
